# version 10
# baseline (speedup 1.0000x reference)
import os
import numpy as np
import concourse.bass as bass
import concourse.mybir as mybir
from concourse.bass_utils import run_bass_kernel_spmd
from concourse.alu_op_type import AluOpType as alu

F32 = mybir.dt.float32
BF16 = mybir.dt.bfloat16
I32 = mybir.dt.int32
AF = mybir.ActivationFunctionType
AX = mybir.AxisListType

B, S, D, H, DH, FF, II = 16, 4096, 1024, 16, 64, 4096, 2816
NC = 8
SBATCH = 8
OB = 2
MAGIC = 12582912.0
QMAX = 127.0
EPS = 1e-5
ROWS8 = SBATCH * S
NT8 = ROWS8 // 512
ROWS2 = OB * S
DC = D // 128
ICN = II // 128
FC1 = FF // 128

KDEBUG = False
LIM = {}


def _lim(name, full):
    return LIM.get(name, full)


def _stat_batches(c):
    p = c // 2
    own = [2 * c, 2 * c + 1]
    pool = [b for b in range(B) if not (4 * p <= b < 4 * p + 4)]
    extra = pool[0:6] if c % 2 == 0 else pool[6:12]
    return own + extra


def build_nc():
    nc = bass.Bass("TRN2", target_bir_lowering=False,
                   detect_race_conditions=False)

    def din(name, shape, dt=F32):
        return nc.dram_tensor(name, shape, dt, kind="ExternalInput").ap()

    x8 = din("x8", [SBATCH, S, D])
    wq = din("wq", [D, D]); wk = din("wk", [D, D]); wv = din("wv", [D, D])
    wo = din("wo", [D, D]); proj = din("proj", [DH, DH])
    w1 = din("w1", [D, FF]); w2 = din("w2", [FF, D])
    wg = din("wg", [II, D]); wu = din("wu", [II, D]); wd = din("wd", [D, II])
    g1T = din("g1T", [128, DC]); beta1T = din("beta1T", [128, DC])
    g2T = din("g2T", [128, DC]); beta2T = din("beta2T", [128, DC])
    b1row_i = din("b1row", [1, FF]); b2row_i = din("b2row", [1, D])
    clip_x = din("clip_x", [1, 1]); clip_dn = din("clip_dn", [1, 1])
    nonce = din("nonce", [1, 2], I32)

    y = nc.dram_tensor("y", [OB, S, D], F32, kind="ExternalOutput").ap()
    dbg = {}
    if KDEBUG:
        for nm, shp in [("dbg_stats_m", [128, 256]), ("dbg_stats_r", [128, 256]),
                        ("dbg_m1", [1, 4]), ("dbg_m2", [1, 4]),
                        ("dbg_sc1", [1, 4]), ("dbg_sc2", [1, 4]),
                        ("dbg_a0", [128, 1024]), ("dbg_a31", [128, 1024]),
                        ("dbg_p0", [128, 1024]), ("dbg_h0", [22, 128, 128])]:
            dbg[nm] = nc.dram_tensor(nm, shp, F32, kind="ExternalOutput").ap()
        dbg["dbg_g0"] = nc.dram_tensor("dbg_g0", [32, 128, 128], BF16,
                                       kind="ExternalOutput").ap()

    a_scr = nc.dram_tensor("a_scr", [ROWS2, D], F32, kind="Internal").ap()
    p_scr = nc.dram_tensor("p_scr", [ROWS2, D], F32, kind="Internal").ap()
    h_scr = nc.dram_tensor("h_scr", [ICN, 128, ROWS2], F32, kind="Internal").ap()
    wgT_scr = nc.dram_tensor("wgT_scr", [DC, 128, II], BF16, kind="Internal").ap()
    wuT_scr = nc.dram_tensor("wuT_scr", [DC, 128, II], BF16, kind="Internal").ap()
    wdT_scr = nc.dram_tensor("wdT_scr", [ICN, 128, D], BF16, kind="Internal").ap()
    swg_scr = nc.dram_tensor("swg_scr", [ICN, 128], F32, kind="Internal").ap()
    swu_scr = nc.dram_tensor("swu_scr", [ICN, 128], F32, kind="Internal").ap()
    swd_scr = nc.dram_tensor("swd_scr", [DC, 128], F32, kind="Internal").ap()
    g_scr = nc.dram_tensor("g_scr", [FC1, 128, ROWS2], BF16,
                           kind="Internal").ap()
    mail = nc.dram_tensor("mail", [1, 16], I32, kind="Internal",
                          addr_space="Shared").ap()

    A = nc.alloc_sbuf_tensor
    stats_m = A("stats_m", [128, 256], F32)
    stats_r = A("stats_r", [128, 256], F32)
    g1_sb = A("g1_sb", [128, DC], F32)
    g2_sb = A("g2_sb", [128, DC], F32)
    b1f_b = A("b1f_b", [128, DC], BF16)
    b2f_b = A("b2f_b", [128, DC], BF16)
    bq_row = A("bq_row", [1, D], BF16)
    bk_row = A("bk_row", [1, D], BF16)
    bv_row = A("bv_row", [1, D], BF16)
    b1p_row = A("b1p_row", [1, FF], BF16)
    b2_bc = A("b2_bc", [128, D], F32)
    swd_bc = A("swd_bc", [128, D], F32)
    onerow = A("onerow", [1, 512], BF16)
    tri_mask = A("tri_mask", [128, 128], F32)
    ident_b = A("ident_b", [128, 128], BF16)
    proj2 = A("proj2", [128, 128], BF16)
    S_sb = A("S_sb", [128, 16 * 65], F32)
    S_b = A("S_b", [128, 16 * 65], BF16)
    V_aug = A("V_aug", [128, 16 * 65], BF16)
    num_sb = A("num_sb", [128, 2 * 520], F32)
    den_sb = A("den_sb", [128, 16], F32)
    rec_sb = A("rec_sb", [128, 16], F32)
    sq_waste = A("sq_waste", [128, 1024], F32)
    st_sum = A("st_sum", [128, 4], F32)
    st_sq = A("st_sq", [128, 4], F32)
    st_mx = A("st_mx", [128, 4], F32)
    st_mn = A("st_mn", [128, 4], F32)
    st_t1 = A("st_t1", [128, 4], F32)
    st_t2 = A("st_t2", [128, 4], F32)
    gmax = A("gmax", [128, 1], F32)
    hmax = A("hmax", [128, ICN], F32)
    fold32 = A("fold32", [1, 128], F32)
    m_loc = A("m_loc", [1, 1], F32)
    m_glob = A("m_glob", [1, 1], F32)
    sc_x = A("sc_x", [1, 4], F32)
    inv_sx_b = A("inv_sx_b", [128, 1], F32)
    sx_b = A("sx_b", [128, 1], F32)
    inv_sd_b = A("inv_sd_b", [128, 1], F32)
    sd_b = A("sd_b", [128, 1], F32)
    rs_all = A("rs_all", [128, 256], F32)
    mrs_all = A("mrs_all", [128, 256], F32)
    swg_sb = A("swg_sb", [128, ICN], F32)
    swu_sb = A("swu_sb", [128, ICN], F32)
    scg_sb = A("scg_sb", [128, ICN], F32)
    scu_sb = A("scu_sb", [128, ICN], F32)
    sendb = A("sendb", [1, 4], I32)
    pollb = A("pollb", [1, 4], I32)
    nonce_sb = A("nonce_sb", [1, 2], I32)
    clip_sb = A("clip_sb", [1, 2], F32)
    a1st = A("a1st", [128, 4], F32)

    SCRATCH = 160 * 1024
    bret = nc.bump_sbuf(SCRATCH)
    assert bret is not None, "sbuf scratch reservation failed"
    scratch_base = bret[0] if isinstance(bret, tuple) else bret

    class Region:
        cnt = 0

        def __init__(self):
            self.off = scratch_base

        def al(self, shape, dt):
            bpp = int(np.prod(shape[1:])) * mybir.dt.size(dt)
            off = (self.off + 31) & ~31
            Region.cnt += 1
            t = nc.alloc_sbuf_tensor_at(f"scr{Region.cnt}", shape, dt,
                                        offset=off)
            self.off = off + bpp
            assert self.off <= scratch_base + SCRATCH, "scratch overflow"
            return t

    P8 = nc.alloc_psum_tensor("P8", [128, 4096], F32)
    P8b = P8.bitcast(BF16)

    def bk(i):
        return P8[:, i * 512:(i + 1) * 512]

    def bkb(i):
        return P8b[:, i * 1024:(i + 1) * 1024]

    def banks_head(nn):
        return P8[:].rearrange("p (b c) -> p b c", c=512)[:, :, 0:nn]

    streams = {k: [] for k in ("sync", "vector", "scalar", "tensor", "gpsimd")}

    class Sem:
        def __init__(self, name):
            self.h = nc.alloc_semaphore(name)
            self.v = 0

        def inc(self, n):
            self.v += n
            return (self.h, n)

        def at(self):
            return (self.h, self.v)

    LD = Sem("LD"); GD = Sem("GD"); TT = Sem("TT"); VV = Sem("VV")
    SS = Sem("SS"); GG = Sem("GG"); ST = Sem("ST")
    PL1 = nc.alloc_semaphore("PL1")
    PL2 = nc.alloc_semaphore("PL2")

    def _hz(ap):
        shape = getattr(ap, "shape", None)
        if shape is None or not hasattr(ap, "stride"):
            return False
        f = 1
        for s in shape[1:]:
            f *= s
        if f < 128:
            return True
        try:
            if ap.stride() == 0:
                return True
        except Exception:
            pass
        return False

    def E(eng, meth, *a, inc=None, wait=None, nodr=False, **kw):
        dr = (not nodr and meth not in ("dma_start", "drain", "nop", "memset")
              and eng in ("vector", "scalar", "gpsimd")
              and (any(_hz(x) for x in a[1:])
                   or any(_hz(v) for v in kw.values())))

        def f(e, _m=meth, _a=a, _k=kw, _i=inc, _w=wait, _d=dr):
            if _w:
                for sh, sv in _w:
                    e.wait_ge(sh, sv)
            if _d:
                e.drain()
            ins = getattr(e, _m)(*_a, **_k)
            if _i:
                ins.then_inc(*_i)
        streams[eng].append(f)

    E("sync", "dma_start", out=g1_sb[:], in_=g1T, inc=LD.inc(16))
    E("sync", "dma_start", out=g2_sb[:], in_=g2T, inc=LD.inc(16))
    E("sync", "dma_start", out=nonce_sb[:], in_=nonce, inc=LD.inc(16))
    E("sync", "dma_start", out=clip_sb[0:1, 0:1], in_=clip_x, inc=LD.inc(16))
    E("sync", "dma_start", out=clip_sb[0:1, 1:2], in_=clip_dn, inc=LD.inc(16))
    E("sync", "dma_start", out=b2_bc[0:1, :], in_=b2row_i, inc=LD.inc(16))
    setup_ld = LD.at()
    k = 1
    while k < 128:
        kk = min(k, 128 - k)
        E("sync", "dma_start", out=b2_bc[k:k + kk, :], in_=b2_bc[0:kk, :],
          wait=[LD.at()], inc=LD.inc(16))
        k *= 2
    b2bc_ld = LD.at()
    E("gpsimd", "memset", tri_mask[:], 1.0)
    E("gpsimd", "affine_select", tri_mask[:], tri_mask[:], [[1, 128]],
      alu.is_ge, 0.0, base=0, channel_multiplier=-1)
    E("gpsimd", "memset", ident_b[:], 1.0)
    E("gpsimd", "affine_select", ident_b[:], ident_b[:], [[1, 128]],
      alu.is_equal, 0.0, base=0, channel_multiplier=-1)
    E("gpsimd", "memset", onerow[:], 1.0)
    E("gpsimd", "memset", gmax[:], 0.0)
    E("gpsimd", "memset", stats_m[:], 0.0)
    E("gpsimd", "memset", stats_r[:], 1.0)
    E("gpsimd", "memset", hmax[:], 0.0)
    E("gpsimd", "memset", proj2[:], 0.0)
    vaug_v = V_aug[:].rearrange("p (h c) -> p h c", h=16)
    E("gpsimd", "memset", vaug_v[:, :, 64:65], 1.0, inc=GG.inc(1))
    E("gpsimd", "dma_start", out=b1f_b[:], in_=beta1T, inc=GD.inc(16))
    E("gpsimd", "dma_start", out=b2f_b[:], in_=beta2T, inc=GD.inc(16))
    E("gpsimd", "dma_start", out=proj2[0:64, 0:64], in_=proj,
      wait=[GG.at()], inc=GD.inc(16))
    proj_gd = GD.at()
    E("vector", "tensor_copy", proj2[64:128, 64:128], proj2[0:64, 0:64],
      wait=[proj_gd], inc=VV.inc(1))

    reg = Region()
    wp_f = [reg.al([128, II], F32) for _ in range(2)]
    wp_i = [reg.al([128, II], BF16) for _ in range(2)]
    wp_t = [reg.al([128, ICN * 128], BF16) for _ in range(2)]
    wp_mx = reg.al([128, 2], F32)
    wp_sc = reg.al([128, 2], F32)

    def prep_weight(w_ap, n_chunks, wT_scr_ap, s_scr_ap, free_len):
        w_t = w_ap.rearrange("(n p) f -> n p f", p=128)
        ntr = free_len // 128
        for ch in range(n_chunks):
            bi = ch % 2
            E("sync", "dma_start", out=wp_f[bi][:, 0:free_len],
              in_=w_t[ch], wait=[VV.at(), GD.at()], inc=LD.inc(16))
            ld = LD.at()
            E("vector", "reduce_max", wp_mx[:, bi:bi + 1],
              wp_f[bi][:, 0:free_len], AX.X, apply_absolute_value=True,
              wait=[ld])
            E("vector", "reciprocal", wp_sc[:, bi:bi + 1], wp_mx[:, bi:bi + 1])
            E("vector", "tensor_scalar", wp_sc[:, bi:bi + 1],
              wp_sc[:, bi:bi + 1], QMAX, None, alu.mult)
            E("vector", "tensor_scalar", wp_mx[:, bi:bi + 1],
              wp_mx[:, bi:bi + 1], 1.0 / QMAX, None, alu.mult, inc=VV.inc(1))
            E("gpsimd", "dma_start",
              out=s_scr_ap[ch:ch + 1, :].rearrange("a p -> p a"),
              in_=wp_mx[:, bi:bi + 1], wait=[VV.at()], inc=GD.inc(16),
              allow_slow_non_contiguous=True)
            E("vector", "scalar_tensor_tensor", wp_f[bi][:, 0:free_len],
              wp_f[bi][:, 0:free_len], 1.0,
              wp_sc[:, bi:bi + 1].broadcast_to((128, free_len)),
              alu.bypass, alu.mult)
            E("vector", "tensor_scalar", wp_f[bi][:, 0:free_len],
              wp_f[bi][:, 0:free_len], MAGIC, MAGIC, alu.add, alu.subtract)
            E("vector", "tensor_scalar", wp_i[bi][:, 0:free_len],
              wp_f[bi][:, 0:free_len], QMAX, -QMAX - 1.0, alu.min, alu.max,
              inc=VV.inc(1))
            vq = VV.at()
            pbank = 4 * bi
            for tb in range(ntr):
                g8 = tb // 8
                bb = pbank + (g8 % 2)
                E("tensor", "transpose",
                  bkb(bb)[:, (tb % 8) * 128:(tb % 8) * 128 + 128],
                  wp_i[bi][:, tb * 128:tb * 128 + 128], ident_b[:],
                  wait=([vq, SS.at()] if tb % 8 == 0 else None),
                  inc=(TT.inc(1) if (tb % 8 == 7 or tb == ntr - 1) else None))
                if tb % 8 == 7 or tb == ntr - 1:
                    lo = g8 * 8
                    n = tb - lo + 1
                    E("scalar", "copy", wp_t[bi][:, lo * 128:(lo + n) * 128],
                      bkb(bb)[:, 0:n * 128], wait=[TT.at()], inc=SS.inc(1))
            E("gpsimd", "dma_start",
              out=wT_scr_ap[0:ntr, :, ch * 128:ch * 128 + 128].rearrange(
                  "n p f -> p n f"),
              in_=wp_t[bi][:, 0:ntr * 128].rearrange("p (n f) -> p n f",
                                                     f=128),
              wait=[SS.at()], inc=GD.inc(16))

    prep_weight(wg, ICN, wgT_scr, swg_scr, D)
    prep_weight(wu, ICN, wuT_scr, swu_scr, D)
    prep_weight(wd, DC, wdT_scr, swd_scr, II)
    prep_gd = GD.at()
    prep_vv = VV.at()
    prep_ss = SS.at()
    prep_tt = TT.at()

    x8t = x8.rearrange("b (t p) d -> (b t) p d", p=128)
    a0x = [reg.al([128, 4096], F32) for _ in range(2)]
    a0_vmark = [None, None]

    for t in range(_lim("NT8", NT8)):
        bi = t % 2
        E("sync", "dma_start",
          out=a0x[bi][:].rearrange("p (a d) -> p a d", d=D),
          in_=x8t[4 * t:4 * t + 4].rearrange("a p d -> p a d"),
          wait=([a0_vmark[bi]] if a0_vmark[bi] else None), inc=LD.inc(16))
        ld = LD.at()
        E("scalar", "nop", wait=[ld, VV.at()])
        E("vector", "nop", wait=[ld])
        for j in range(4):
            xx = a0x[bi][:, j * D:(j + 1) * D]
            E("scalar", "activation", sq_waste[:], xx, AF.Square,
              accum_out=st_sq[:, j:j + 1])
        for j in range(4):
            xx = a0x[bi][:, j * D:(j + 1) * D]
            E("scalar", "activation", sq_waste[:], xx, AF.Identity,
              accum_out=st_sum[:, j:j + 1])
            E("vector", "reduce_max", st_mx[:, j:j + 1], xx, AX.X)
            E("vector", "tensor_reduce", st_mn[:, j:j + 1], xx, AX.X, alu.min)
        E("scalar", "drain", inc=SS.inc(1))
        sq_done = SS.at()
        cols = slice(4 * t, 4 * t + 4)
        E("vector", "tensor_scalar", stats_m[:, cols], st_sum[:], 1.0 / D,
          None, alu.mult, wait=[sq_done])
        E("vector", "tensor_scalar", st_sq[:], st_sq[:], 1.0 / D, None,
          alu.mult)
        E("vector", "tensor_mul", st_t1[:], stats_m[:, cols], stats_m[:, cols])
        E("vector", "tensor_sub", st_sq[:], st_sq[:], st_t1[:])
        E("vector", "tensor_scalar", st_sq[:], st_sq[:], EPS, None, alu.add,
          inc=VV.inc(1))
        E("scalar", "activation", st_sq[:], st_sq[:], AF.Sqrt,
          wait=[VV.at()], inc=SS.inc(1))
        E("vector", "reciprocal", stats_r[:, cols], st_sq[:], wait=[SS.at()])
        E("vector", "tensor_sub", st_t1[:], st_mx[:], stats_m[:, cols])
        E("vector", "tensor_sub", st_t2[:], stats_m[:, cols], st_mn[:])
        E("vector", "tensor_max", st_t1[:], st_t1[:], st_t2[:])
        E("vector", "tensor_mul", st_t1[:], st_t1[:], stats_r[:, cols])
        E("vector", "reduce_max", st_t2[:, 0:1], st_t1[:], AX.X)
        E("vector", "tensor_max", gmax[:], gmax[:], st_t2[:, 0:1],
          inc=VV.inc(1))
        a0_vmark[bi] = VV.at()

    def fold_max():
        E("sync", "dma_start", out=fold32[:], in_=gmax[:, 0:1],
          wait=[VV.at()], inc=LD.inc(16))
        E("vector", "reduce_max", m_loc[:], fold32[:], AX.X,
          wait=[LD.at()], inc=VV.inc(1))

    def pair_exchange(slot_base, poll_sem, tag):
        E("vector", "tensor_copy", sendb[0:1, 0:2], nonce_sb[0:1, 0:2],
          wait=[setup_ld])
        E("vector", "tensor_copy", sendb[0:1, 2:3],
          m_loc[0:1, 0:1].bitcast(I32))
        E("vector", "memset", sendb[0:1, 3:4], 0, inc=VV.inc(1))
        vready = VV.at()

        def g_exchange(g, _vr=vready, _sb=slot_base, _ps=poll_sem, _t=tag):
            g.wait_ge(*_vr)
            cid = g.partition_id()
            with (g.register(f"par{_t}") as rpar,
                  g.register(f"rnot{_t}") as rnot,
                  g.register(f"rcnt{_t}") as rcnt,
                  g.register(f"rn0{_t}") as rn0,
                  g.register(f"rn1{_t}") as rn1,
                  g.register(f"rf0{_t}") as rf0,
                  g.register(f"rf1{_t}") as rf1,
                  g.register(f"rd0{_t}") as rd0,
                  g.register(f"rd1{_t}") as rd1):
                g.reg_mov(rpar, cid % 2)
                g.reg_load(rn0, nonce_sb[0:1, 0:1])
                g.reg_load(rn1, nonce_sb[0:1, 1:2])
                g.reg_mov(rnot, 1)
                g.reg_mov(rcnt, 0)
                for par in (0, 1):
                    my = _sb + 4 * par
                    ot = _sb + 4 * (1 - par)
                    with g.If_eq(rpar, par):
                        g.dma_start(out=mail[0:1, my + 2:my + 4],
                                    in_=sendb[0:1, 2:4]).then_inc(_ps, 16)
                        g.reg_add(rcnt, rcnt, 16)
                        g.wait_ge(_ps, rcnt)
                        g.dma_start(out=mail[0:1, my:my + 2],
                                    in_=sendb[0:1, 0:2]).then_inc(_ps, 16)
                        g.reg_add(rcnt, rcnt, 16)
                        g.wait_ge(_ps, rcnt)
                        with g.While(rnot):
                            g.reg_add(rcnt, rcnt, 16)
                            g.dma_start(out=pollb[0:1, 0:4],
                                        in_=mail[0:1, ot:ot + 4]
                                        ).then_inc(_ps, 16)
                            g.wait_ge(_ps, rcnt)
                            g.reg_load(rf0, pollb[0:1, 0:1])
                            g.reg_load(rf1, pollb[0:1, 1:2])
                            g.reg_alu(rd0, rf0, rn0, alu.bitwise_xor)
                            g.reg_alu(rd1, rf1, rn1, alu.bitwise_xor)
                            g.reg_alu(rnot, rd0, rd1, alu.bitwise_or)
        streams["gpsimd"].append(g_exchange)
        E("gpsimd", "drain", inc=GG.inc(1))
        return GG.at()

    def exchange_consume(gg_mark):
        E("vector", "tensor_tensor", m_glob[0:1, 0:1], m_loc[0:1, 0:1],
          pollb[0:1, 2:3].bitcast(F32), alu.max, wait=[gg_mark],
          inc=VV.inc(1))

    def scale_pipeline(clip_col, s_bc, inv_bc):
        E("scalar", "activation", sc_x[0:1, 0:1],
          clip_sb[0:1, clip_col:clip_col + 1], AF.Sigmoid,
          wait=[VV.at()], inc=SS.inc(1))
        E("vector", "tensor_mul", sc_x[0:1, 1:2], sc_x[0:1, 0:1],
          m_glob[0:1, 0:1], wait=[SS.at()])
        E("vector", "tensor_scalar", sc_x[0:1, 1:2], sc_x[0:1, 1:2],
          1.0 / QMAX, None, alu.mult)
        E("vector", "reciprocal", sc_x[0:1, 2:3], sc_x[0:1, 1:2],
          inc=VV.inc(1))
        E("sync", "dma_start", out=s_bc[0:1, :], in_=sc_x[0:1, 1:2],
          wait=[VV.at()], inc=LD.inc(16))
        E("sync", "dma_start", out=inv_bc[0:1, :], in_=sc_x[0:1, 2:3],
          inc=LD.inc(16))
        k2 = 1
        while k2 < 128:
            kk2 = min(k2, 128 - k2)
            E("sync", "dma_start", out=s_bc[k2:k2 + kk2, :],
              in_=s_bc[0:kk2, :], wait=[LD.at()], inc=LD.inc(16))
            E("sync", "dma_start", out=inv_bc[k2:k2 + kk2, :],
              in_=inv_bc[0:kk2, :], inc=LD.inc(16))
            k2 *= 2

    fold_max()
    x1_gg = pair_exchange(0, PL1, "x1")

    reg = Region()
    wq_sb = reg.al([128, DC * 1024], BF16)
    wk_sb = reg.al([128, DC * 1024], BF16)
    wv_sb = reg.al([128, DC * 1024], BF16)
    wo_sb = reg.al([128, DC * 1024], BF16)
    a1_stage = [reg.al([128, 1024], F32) for _ in range(2)]
    x_t = reg.al([128, 1024], F32)
    xn_t = reg.al([128, 1024], F32)
    a_t = reg.al([128, 1024], F32)
    h1_t = reg.al([128, 1024], BF16)
    h1f_t = reg.al([128, 1024], BF16)
    qt_sb = reg.al([128, 1024], BF16)
    kt_sb = reg.al([128, 1024], BF16)
    Q_sb = reg.al([128, 1024], BF16)
    K_sb = reg.al([128, 1024], BF16)
    Ktm_sb = reg.al([128, 1024], BF16)
    A_sb = reg.al([128, 2048], BF16)
    at_fm = reg.al([128, 1024], BF16)
    atn_b = reg.al([128, 1024], BF16)
    phi_e = reg.al([128, 1024], F32)
    phi_r = reg.al([128, 1024], F32)
    phi_k_e = reg.al([128, 1024], F32)
    phi_k_r = reg.al([128, 1024], F32)
    brow_f = reg.al([1, 1024], F32)

    def load_w_and_bias(w_ap, dst_sb, brow_b, gcol, first):
        wt = w_ap.rearrange("(n p) f -> n p f", p=128)
        for ch in range(DC):
            bi = ch % 2
            wl = [VV.at()]
            if first and ch == 0:
                wl += [prep_gd, prep_vv, prep_ss, prep_tt, b2bc_ld]
            E("sync", "dma_start", out=a1_stage[bi][:], in_=wt[ch],
              wait=wl, inc=LD.inc(16))
            E("vector", "tensor_copy", dst_sb[:, ch * 1024:(ch + 1) * 1024],
              a1_stage[bi][:], wait=[LD.at()], inc=VV.inc(1))
        wcast = VV.at()
        for hh in range(2):
            for ch in range(DC):
                E("tensor", "matmul", bk(hh)[0:1, :],
                  b1f_b[:, ch:ch + 1],
                  dst_sb[:, ch * 1024 + hh * 512:ch * 1024 + (hh + 1) * 512],
                  start=(ch == 0), stop=(ch == DC - 1),
                  wait=([wcast, VV.at()] if ch == 0 else None),
                  inc=(TT.inc(1) if ch == DC - 1 else None))
        E("vector", "tensor_copy", brow_f[0:1, 0:512], bk(0)[0:1, :],
          wait=[TT.at()])
        E("vector", "tensor_copy", brow_f[0:1, 512:1024], bk(1)[0:1, :])
        E("vector", "tensor_copy", brow_b[0:1, :], brow_f[0:1, :],
          inc=VV.inc(1))
        for ch in range(DC):
            E("vector", "tensor_scalar", dst_sb[:, ch * 1024:(ch + 1) * 1024],
              dst_sb[:, ch * 1024:(ch + 1) * 1024], gcol[:, ch:ch + 1],
              None, alu.mult)
        E("vector", "drain", inc=VV.inc(1))

    load_w_and_bias(wq, wq_sb, bq_row, g1_sb, True)
    load_w_and_bias(wk, wk_sb, bk_row, g1_sb, False)
    load_w_and_bias(wv, wv_sb, bv_row, g1_sb, False)
    a1w_vv = VV.at()
    E("gpsimd", "dma_start",
      out=wo_sb[:].rearrange("p (n f) -> p n f", f=1024),
      in_=wo.rearrange("(n p) f -> p n f", p=128), wait=[a1w_vv],
      inc=GD.inc(16))
    a1_wo_gd = GD.at()

    x2t = x8.rearrange("b (t p) d -> (b t) p d", p=128)
    a_scr_t = a_scr.rearrange("(t p) d -> t p d", p=128)
    NCHUNK = S // 128
    astore = {"m": None}

    def a1_chunk(gtile):
        sc = gtile
        E("sync", "dma_start", out=x_t[:], in_=x2t[gtile],
          wait=[VV.at()], inc=LD.inc(16))
        ld = LD.at()
        E("vector", "tensor_scalar", xn_t[:], x_t[:],
          stats_m[:, sc:sc + 1], stats_r[:, sc:sc + 1],
          alu.subtract, alu.mult, wait=[ld], inc=VV.inc(1))
        xn_done = VV.at()
        E("scalar", "activation", sq_waste[:], xn_t[:], AF.Square,
          accum_out=a1st[:, 1:2], wait=[xn_done], inc=SS.inc(1))
        E("vector", "reduce_sum", a1st[:, 0:1], xn_t[:], AX.X)
        E("vector", "tensor_scalar", a1st[:, 0:1], a1st[:, 0:1], 1.0 / D,
          None, alu.mult)
        E("vector", "tensor_scalar", a1st[:, 1:2], a1st[:, 1:2], 1.0 / D,
          None, alu.mult, wait=[SS.at()])
        E("vector", "tensor_mul", a1st[:, 2:3], a1st[:, 0:1], a1st[:, 0:1])
        E("vector", "tensor_sub", a1st[:, 1:2], a1st[:, 1:2], a1st[:, 2:3])
        E("vector", "tensor_scalar", a1st[:, 1:2], a1st[:, 1:2], EPS, None,
          alu.add, inc=VV.inc(1))
        E("scalar", "activation", a1st[:, 1:2], a1st[:, 1:2], AF.Sqrt,
          wait=[VV.at()], inc=SS.inc(1))
        E("vector", "reciprocal", a1st[:, 3:4], a1st[:, 1:2], wait=[SS.at()])
        E("vector", "tensor_scalar", h1_t[:], xn_t[:], a1st[:, 0:1],
          a1st[:, 3:4], alu.subtract, alu.mult, inc=VV.inc(1))
        h1_done = VV.at()
        if _lim("A1W", 99) < 1:
            return
        for fc in range(DC):
            E("tensor", "transpose", bkb(0)[:, fc * 128:fc * 128 + 128],
              h1_t[:, fc * 128:fc * 128 + 128], ident_b[:],
              wait=([h1_done, SS.at()] if fc == 0 else None),
              inc=(TT.inc(1) if fc == DC - 1 else None))
        E("scalar", "copy", h1f_t[:], bkb(0)[:, 0:1024], wait=[TT.at()],
          inc=SS.inc(1))
        h1f_done = SS.at()
        if _lim("A1W", 99) < 2:
            return
        for fc in range(DC):
            qcol = 1024 + fc * 128
            for dc in range(DC):
                E("tensor", "matmul", P8[:, qcol:qcol + 128],
                  wq_sb[:, dc * 1024 + fc * 128:dc * 1024 + fc * 128 + 128],
                  h1f_t[:, dc * 128:dc * 128 + 128],
                  start=(dc == 0), stop=False,
                  wait=([h1f_done] if (fc == 0 and dc == 0) else None))
            E("tensor", "matmul", P8[:, qcol:qcol + 128],
              bq_row[0:1, fc * 128:fc * 128 + 128], onerow[0:1, 0:128],
              start=False, stop=True,
              inc=(TT.inc(1) if fc == DC - 1 else None))
        qwave = TT.at()
        E("scalar", "copy", qt_sb[:], P8[:, 1024:2048],
          wait=[qwave], inc=SS.inc(1))
        q_evac = SS.at()
        if _lim("A1W", 99) < 3:
            return
        for fc in range(DC):
            kcol = 2048 + fc * 128
            for dc in range(DC):
                E("tensor", "matmul", P8[:, kcol:kcol + 128],
                  wk_sb[:, dc * 1024 + fc * 128:dc * 1024 + fc * 128 + 128],
                  h1f_t[:, dc * 128:dc * 128 + 128],
                  start=(dc == 0), stop=False)
            E("tensor", "matmul", P8[:, kcol:kcol + 128],
              bk_row[0:1, fc * 128:fc * 128 + 128], onerow[0:1, 0:128],
              start=False, stop=True,
              inc=(TT.inc(1) if fc == DC - 1 else None))
        kwave = TT.at()
        E("scalar", "copy", kt_sb[:], P8[:, 2048:3072],
          wait=[kwave], inc=SS.inc(1))
        k_evac = SS.at()
        if _lim("A1W", 99) < 4:
            return
        for hh in range(2):
            for dc in range(DC):
                E("tensor", "matmul", bk(6 + hh),
                  h1f_t[:, dc * 128:dc * 128 + 128],
                  wv_sb[:, dc * 1024 + hh * 512:dc * 1024 + (hh + 1) * 512],
                  start=(dc == 0), stop=False)
            E("tensor", "matmul", bk(6 + hh), onerow[0:1, 0:128],
              bv_row[0:1, hh * 512:(hh + 1) * 512], start=False, stop=True,
              inc=(TT.inc(1) if hh == 1 else None))
        vwave = TT.at()
        E("vector", "tensor_copy", vaug_v[:, :, 0:64],
          P8[:, 3072:4096].rearrange("p (h c) -> p h c", c=64),
          wait=[vwave], inc=VV.inc(1))
        v_evac = VV.at()
        if _lim("A1W", 99) < 5:
            return
        for fc in range(DC):
            E("tensor", "matmul", P8[:, fc * 128:fc * 128 + 128], proj2[:],
              qt_sb[:, fc * 128:fc * 128 + 128], start=True, stop=True,
              wait=([q_evac] if fc == 0 else None),
              inc=(TT.inc(1) if fc == DC - 1 else None))
        qproj = TT.at()
        E("scalar", "activation", phi_e[:], P8[:, 0:1024],
          AF.Exp, wait=[qproj], inc=SS.inc(1))
        E("vector", "tensor_scalar", phi_r[:], P8[:, 0:1024],
          0.0, None, alu.max, wait=[SS.at()])
        E("vector", "scalar_tensor_tensor", Q_sb[:], phi_e[:], 1.0,
          phi_r[:], alu.min, alu.add, inc=VV.inc(1))
        q_phi = VV.at()
        if _lim("A1W", 99) < 6:
            return
        for fc in range(DC):
            E("tensor", "matmul", P8[:, 1024 + fc * 128:1024 + fc * 128 + 128],
              proj2[:],
              kt_sb[:, fc * 128:fc * 128 + 128], start=True, stop=True,
              wait=([k_evac] if fc == 0 else None),
              inc=(TT.inc(1) if fc == DC - 1 else None))
        kproj = TT.at()
        E("scalar", "activation", phi_k_e[:], P8[:, 1024:2048],
          AF.Exp, wait=[kproj], inc=SS.inc(1))
        E("vector", "tensor_scalar", phi_k_r[:], P8[:, 1024:2048],
          0.0, None, alu.max, wait=[SS.at()])
        E("vector", "scalar_tensor_tensor", K_sb[:], phi_k_e[:], 1.0,
          phi_k_r[:], alu.min, alu.add, inc=VV.inc(1))
        k_phi = VV.at()
        if _lim("A1W", 99) < 7:
            return
        for fc in range(DC):
            E("tensor", "transpose", bkb(0)[:, fc * 128:fc * 128 + 128],
              K_sb[:, fc * 128:fc * 128 + 128], ident_b[:],
              wait=([k_phi] if fc == 0 else None),
              inc=(TT.inc(1) if fc == DC - 1 else None))
        E("scalar", "copy", Ktm_sb[:], bkb(0)[:, 0:1024], wait=[TT.at()],
          inc=SS.inc(1))
        ktm_done = SS.at()
        if _lim("A1W", 99) < 8:
            return
        for rnd in range(2):
            for hh8 in range(8):
                hh = rnd * 8 + hh8
                E("tensor", "matmul", bk(hh8)[:, 0:128],
                  K_sb[64 * (hh % 2):64 * (hh % 2) + 64,
                       (hh // 2) * 128:(hh // 2) * 128 + 128],
                  Q_sb[64 * (hh % 2):64 * (hh % 2) + 64,
                       (hh // 2) * 128:(hh // 2) * 128 + 128],
                  start=True, stop=True,
                  wait=([ktm_done] if (rnd == 0 and hh8 == 0) else
                        ([VV.at()] if hh8 == 0 else None)),
                  inc=(TT.inc(1) if hh8 == 7 else None))
            aw = TT.at()
            E("vector", "scalar_tensor_tensor",
              A_sb[:, rnd * 1024:(rnd + 1) * 1024].rearrange(
                  "p (b c) -> p b c", c=128),
              banks_head(128), 1.0,
              tri_mask[:].unsqueeze(1).broadcast_to((128, 8, 128)),
              alu.bypass, alu.mult, wait=[aw], inc=VV.inc(1))
        aevac = VV.at()
        if _lim("A1W", 99) < 9:
            return
        for rnd in range(2):
            for hh8 in range(8):
                hh = rnd * 8 + hh8
                E("tensor", "matmul", bk(hh8)[:, 0:65],
                  A_sb[:, hh * 128:hh * 128 + 128],
                  V_aug[:, hh * 65:hh * 65 + 65],
                  start=True, stop=False,
                  wait=([aevac] if (rnd == 0 and hh8 == 0) else
                        ([VV.at()] if hh8 == 0 else None)))
                E("tensor", "matmul", bk(hh8)[:, 0:65],
                  Q_sb[64 * (hh % 2):64 * (hh % 2) + 64,
                       (hh // 2) * 128:(hh // 2) * 128 + 128],
                  S_b[64 * (hh % 2):64 * (hh % 2) + 64,
                      hh * 65:hh * 65 + 65],
                  start=False, stop=True,
                  inc=(TT.inc(1) if hh8 == 7 else None))
            nw = TT.at()
            E("vector", "tensor_copy",
              num_sb[:, rnd * 520:(rnd + 1) * 520].rearrange(
                  "p (b c) -> p b c", c=65),
              banks_head(65), wait=[nw], inc=VV.inc(1))
        numv = num_sb[:].rearrange("p (b c) -> p b c", c=65)
        E("vector", "tensor_copy", den_sb[:], numv[:, :, 64:65].squeeze(2))
        E("vector", "tensor_scalar", den_sb[:], den_sb[:], 1e-6, None,
          alu.add)
        E("vector", "reciprocal", rec_sb[:], den_sb[:])
        E("vector", "tensor_tensor",
          phi_e[:].rearrange("p (h c) -> p h c", c=64),
          numv[:, :, 0:64],
          rec_sb[:].unsqueeze(2).broadcast_to((128, 16, 64)),
          alu.mult)
        E("vector", "tensor_copy", atn_b[:], phi_e[:], inc=VV.inc(1))
        attn_done = VV.at()
        if _lim("A1W", 99) < 10:
            return
        for rnd in range(2):
            for pj4 in range(4):
                pj = rnd * 4 + pj4
                for par in range(2):
                    hh = 2 * pj + par
                    E("tensor", "matmul", bk(2 * pj4 + par)[:, 0:65],
                      Ktm_sb[:, pj * 128:pj * 128 + 128],
                      V_aug[:, hh * 65:hh * 65 + 65],
                      start=True, stop=True,
                      wait=([attn_done] if (rnd == 0 and pj4 == 0 and par == 0)
                            else ([VV.at()] if (pj4 == 0 and par == 0)
                                  else None)),
                      inc=(TT.inc(1) if (pj4 == 3 and par == 1) else None))
            dw = TT.at()
            E("vector", "tensor_add",
              S_sb[:, rnd * 520:(rnd + 1) * 520].rearrange(
                  "p (b c) -> p b c", c=65),
              S_sb[:, rnd * 520:(rnd + 1) * 520].rearrange(
                  "p (b c) -> p b c", c=65),
              banks_head(65), wait=[dw], inc=VV.inc(1))
        E("vector", "tensor_copy", S_b[:], S_sb[:], inc=VV.inc(1))
        supd = VV.at()
        if _lim("A1W", 99) < 11:
            return
        for fc in range(DC):
            E("tensor", "transpose", bkb(0)[:, fc * 128:fc * 128 + 128],
              atn_b[:, fc * 128:fc * 128 + 128], ident_b[:],
              wait=([supd] if fc == 0 else None),
              inc=(TT.inc(1) if fc == DC - 1 else None))
        E("scalar", "copy", at_fm[:], bkb(0)[:, 0:1024], wait=[TT.at()],
          inc=SS.inc(1))
        atfm = SS.at()
        if _lim("A1W", 99) < 12:
            return
        for hh in range(2):
            for dc in range(DC):
                E("tensor", "matmul", bk(hh),
                  at_fm[:, dc * 128:dc * 128 + 128],
                  wo_sb[:, dc * 1024 + hh * 512:dc * 1024 + (hh + 1) * 512],
                  start=(dc == 0), stop=(dc == DC - 1),
                  wait=([atfm, a1_wo_gd] if (hh == 0 and dc == 0) else None))
        E("tensor", "drain", inc=TT.inc(1))
        wopsum = TT.at()
        wl = [wopsum] + ([astore["m"]] if astore["m"] else [])
        E("vector", "tensor_add", a_t[:, 0:512], xn_t[:, 0:512], bk(0),
          wait=wl)
        E("vector", "tensor_add", a_t[:, 512:1024], xn_t[:, 512:1024],
          bk(1), inc=VV.inc(1))
        E("sync", "dma_start", out=a_scr_t[gtile], in_=a_t[:],
          wait=[VV.at()], inc=LD.inc(16))
        astore["m"] = LD.at()

    for seq in range(_lim("OB", OB)):
        E("vector", "memset", S_sb[:], 0.0)
        E("vector", "memset", S_b[:], 0.0, inc=VV.inc(1))
        for ch in range(_lim("NCHUNK", NCHUNK)):
            a1_chunk(seq * NCHUNK + ch)
    a1_last_ld = LD.at()

    exchange_consume(x1_gg)
    scale_pipeline(0, sx_b, inv_sx_b)
    sx_ld = LD.at()
    if KDEBUG:
        E("sync", "dma_start", out=dbg["dbg_sc1"], in_=sc_x[0:1, 0:4],
          wait=[sx_ld], inc=ST.inc(16))
    E("vector", "scalar_tensor_tensor", rs_all[:], stats_r[:], 1.0,
      inv_sx_b[:, 0:1].broadcast_to((128, 256)), alu.bypass, alu.mult,
      wait=[sx_ld])
    E("vector", "tensor_mul", mrs_all[:], stats_m[:], rs_all[:],
      inc=VV.inc(1))
    x1_vv = VV.at()

    if KDEBUG:
        E("sync", "dma_start", out=dbg["dbg_stats_m"], in_=stats_m[:],
          wait=[x1_vv], inc=ST.inc(16))
        E("sync", "dma_start", out=dbg["dbg_stats_r"], in_=stats_r[:],
          inc=ST.inc(16))
        E("vector", "tensor_copy", st_t1[0:1, 0:1], m_loc[0:1, 0:1])
        E("vector", "tensor_copy", st_t1[0:1, 1:2], m_glob[0:1, 0:1],
          inc=VV.inc(1))
        E("sync", "dma_start", out=dbg["dbg_m1"], in_=st_t1[0:1, 0:4],
          wait=[VV.at()], inc=ST.inc(16))

    reg = Region()
    w1_sb = reg.al([128, DC * FF], BF16)
    a2_stage = reg.al([128, FF // 2], F32)
    a2_a = reg.al([128, 2048], F32)
    a2_h2 = reg.al([128, 2048], BF16)
    a2_h2f = reg.al([128, DC * 256], BF16)
    a2_g = reg.al([128, FC1 * 256], BF16)
    a2_st = reg.al([128, 8], F32)
    b1p_f = reg.al([1, FF], F32)

    E("sync", "dma_start", out=b1p_f[0:1, :], in_=b1row_i,
      wait=[a1_last_ld, VV.at()], inc=LD.inc(16))
    b1_ld = LD.at()
    w1t = w1.rearrange("(n p) (h f) -> n h p f", p=128, h=2)
    for ch in range(DC):
        for hf in range(2):
            E("sync", "dma_start", out=a2_stage[:], in_=w1t[ch, hf],
              wait=[VV.at()], inc=LD.inc(16))
            E("vector", "tensor_copy",
              w1_sb[:, ch * FF + hf * 2048:ch * FF + (hf + 1) * 2048],
              a2_stage[:], wait=[LD.at()], inc=VV.inc(1))
    w1cast = VV.at()
    for hh in range(8):
        for ch in range(DC):
            E("tensor", "matmul", bk(hh)[0:1, :],
              b2f_b[:, ch:ch + 1],
              w1_sb[:, ch * FF + hh * 512:ch * FF + (hh + 1) * 512],
              start=(ch == 0), stop=(ch == DC - 1),
              wait=([w1cast, VV.at()] if (hh == 0 and ch == 0) else None),
              inc=(TT.inc(1) if (hh == 7 and ch == DC - 1) else None))
    E("vector", "tensor_add", b1p_f[0:1, :], b1p_f[0:1, :],
      P8[0:1, 0:4096], wait=[TT.at(), b1_ld])
    E("vector", "tensor_copy", b1p_row[0:1, :], b1p_f[0:1, :],
      inc=VV.inc(1))
    for ch in range(DC):
        E("vector", "tensor_scalar", w1_sb[:, ch * FF:(ch + 1) * FF],
          w1_sb[:, ch * FF:(ch + 1) * FF], g2_sb[:, ch:ch + 1],
          None, alu.mult)
    E("vector", "drain", inc=VV.inc(1))

    p_scr_t = p_scr.rearrange("(t p) d -> t p d", p=128)
    g_scr_t = g_scr
    NT_A2 = ROWS2 // 256
    a2_gmark = {}

    for t in range(_lim("NT_A2", NT_A2)):
        wl = [VV.at()]
        if t >= 1 and (t - 1) in a2_gmark:
            wl.append(a2_gmark[t - 1])
        E("sync", "dma_start",
          out=a2_a[:].rearrange("p (a d) -> p a d", d=D),
          in_=a_scr_t[2 * t:2 * t + 2].rearrange("a p d -> p a d"), wait=wl, inc=LD.inc(16))
        ld = LD.at()
        E("scalar", "nop", wait=[ld, VV.at()])
        for j in range(2):
            aa = a2_a[:, j * D:(j + 1) * D]
            E("scalar", "activation", sq_waste[:], aa, AF.Square,
              accum_out=a2_st[:, 4 * j + 1:4 * j + 2], inc=SS.inc(1))
            E("vector", "reduce_sum", a2_st[:, 4 * j:4 * j + 1], aa, AX.X,
              wait=([ld] if j == 0 else None))
            E("vector", "tensor_scalar", a2_st[:, 4 * j:4 * j + 1],
              a2_st[:, 4 * j:4 * j + 1], 1.0 / D, None, alu.mult)
            E("vector", "tensor_scalar", a2_st[:, 4 * j + 1:4 * j + 2],
              a2_st[:, 4 * j + 1:4 * j + 2], 1.0 / D, None, alu.mult,
              wait=[SS.at()])
            E("vector", "tensor_mul", a2_st[:, 4 * j + 2:4 * j + 3],
              a2_st[:, 4 * j:4 * j + 1], a2_st[:, 4 * j:4 * j + 1])
            E("vector", "tensor_sub", a2_st[:, 4 * j + 1:4 * j + 2],
              a2_st[:, 4 * j + 1:4 * j + 2], a2_st[:, 4 * j + 2:4 * j + 3])
            E("vector", "tensor_scalar", a2_st[:, 4 * j + 1:4 * j + 2],
              a2_st[:, 4 * j + 1:4 * j + 2], EPS, None, alu.add,
              inc=VV.inc(1))
            E("scalar", "activation", a2_st[:, 4 * j + 1:4 * j + 2],
              a2_st[:, 4 * j + 1:4 * j + 2], AF.Sqrt, wait=[VV.at()],
              inc=SS.inc(1))
            E("vector", "reciprocal", a2_st[:, 4 * j + 3:4 * j + 4],
              a2_st[:, 4 * j + 1:4 * j + 2], wait=[SS.at()])
            E("vector", "tensor_scalar", a2_h2[:, j * D:(j + 1) * D], aa,
              a2_st[:, 4 * j:4 * j + 1], a2_st[:, 4 * j + 3:4 * j + 4],
              alu.subtract, alu.mult, inc=VV.inc(1))
        h2_done = VV.at()
        for fc in range(DC):
            for j in range(2):
                tb = fc * 2 + j
                E("tensor", "transpose", P8b[:, tb * 128:tb * 128 + 128],
                  a2_h2[:, j * D + fc * 128:j * D + fc * 128 + 128],
                  ident_b[:],
                  wait=([h2_done, SS.at()] if tb == 0 else None),
                  inc=(TT.inc(1) if tb == 15 else None))
        E("scalar", "copy", a2_h2f[:], P8b[:, 0:2048], wait=[TT.at()],
          inc=SS.inc(1))
        h2f_done = SS.at()
        ss_marks = {}
        for fc in range(FC1):
            bb = 2 + (fc % 6)
            wl2 = [h2f_done] if fc < 6 else [ss_marks[fc - 6]]
            for dc in range(DC):
                E("tensor", "matmul", bk(bb)[:, 0:256],
                  w1_sb[:, dc * FF + fc * 128:dc * FF + fc * 128 + 128],
                  a2_h2f[:, dc * 256:(dc + 1) * 256],
                  start=(dc == 0), stop=False,
                  wait=(wl2 if dc == 0 else None))
            E("tensor", "matmul", bk(bb)[:, 0:256],
              b1p_row[0:1, fc * 128:fc * 128 + 128], onerow[0:1, 0:256],
              start=False, stop=True, inc=TT.inc(1))
            E("scalar", "activation", a2_g[:, fc * 256:(fc + 1) * 256],
              bk(bb)[:, 0:256], AF.Gelu, wait=[TT.at()], inc=SS.inc(1))
            ss_marks[fc] = SS.at()
        gelu_done = SS.at()
        E("gpsimd", "dma_start",
          out=g_scr_t[:, :, t * 256:(t + 1) * 256].rearrange(
              "n p f -> p n f"),
          in_=a2_g[:].rearrange("p (n f) -> p n f", f=256),
          wait=[gelu_done], inc=GD.inc(16))
        a2_gmark[t] = GD.at()
    a2a_gd = GD.at()

    reg = Region()
    w2_sb = reg.al([128, FC1 * 1024], BF16)
    b_g = [reg.al([128, FC1 * 128], BF16) for _ in range(2)]
    b_a2x = reg.al([128, 2048], F32)
    b_a2a = reg.al([128, 2048], F32)

    E("gpsimd", "dma_start",
      out=w2_sb[:].rearrange("p (n f) -> p n f", f=1024),
      in_=w2.rearrange("(n p) f -> p n f", p=128), wait=[a2a_gd, VV.at()],
      inc=GD.inc(16))
    a2w_gd = GD.at()
    a2_pstore = {"m": None}
    gld = {}

    for t in range(_lim("NT_A2", NT_A2)):
        wl = [VV.at()]
        if t == 0:
            wl.append(a2a_gd)
        if a2_pstore["m"]:
            wl.append(a2_pstore["m"])
        E("sync", "dma_start",
          out=b_a2a[:].rearrange("p (a d) -> p a d", d=D),
          in_=a_scr_t[2 * t:2 * t + 2].rearrange("a p d -> p a d"), wait=wl, inc=LD.inc(16))
        E("sync", "dma_start",
          out=b_a2x[:].rearrange("p (a d) -> p a d", d=D),
          in_=x2t[2 * t:2 * t + 2].rearrange("a p d -> p a d"), inc=LD.inc(16))
        for j in range(2):
            bi = (2 * t + j) % 2
            wlg = [VV.at()] if (2 * t + j) >= 2 else [VV.at()]
            E("sync", "dma_start",
              out=b_g[bi][:].rearrange("p (n f) -> p n f", f=128),
              in_=g_scr_t[:, :, (2 * t + j) * 128:(2 * t + j + 1) * 128
                          ].rearrange("n p f -> p n f"),
              wait=wlg, inc=LD.inc(16))
        ld = LD.at()
        for j in range(2):
            bi = (2 * t + j) % 2
            for hh in range(2):
                for fc in range(FC1):
                    E("tensor", "matmul", bk(hh),
                      b_g[bi][:, fc * 128:(fc + 1) * 128],
                      w2_sb[:, fc * 1024 + hh * 512:fc * 1024 + (hh + 1) * 512],
                      start=(fc == 0), stop=(fc == FC1 - 1),
                      wait=([ld, a2w_gd, VV.at()]
                            if (hh == 0 and fc == 0) else None))
            E("tensor", "drain", inc=TT.inc(1))
            w2ps = TT.at()
            E("vector", "tensor_add", b_a2x[:, j * D:(j + 1) * D],
              b_a2x[:, j * D:(j + 1) * D], b2_bc[:], wait=[w2ps, b2bc_ld])
            E("vector", "tensor_add", b_a2x[:, j * D:(j + 1) * D],
              b_a2x[:, j * D:(j + 1) * D], b_a2a[:, j * D:(j + 1) * D])
            E("vector", "tensor_add", b_a2x[:, j * D:j * D + 512],
              b_a2x[:, j * D:j * D + 512], bk(0))
            E("vector", "tensor_add", b_a2x[:, j * D + 512:(j + 1) * D],
              b_a2x[:, j * D + 512:(j + 1) * D], bk(1), inc=VV.inc(1))
            E("sync", "dma_start", out=p_scr_t[2 * t + j],
              in_=b_a2x[:, j * D:(j + 1) * D], wait=[VV.at()],
              inc=LD.inc(16))
        a2_pstore["m"] = LD.at()
    a2_done = LD.at()

    reg = Region()
    wgT_sb = reg.al([128, DC * II], BF16)
    wuT_sb = reg.al([128, DC * II], BF16)
    b_x = [reg.al([128, 4096], F32) for _ in range(2)]
    b_xq = reg.al([128, 4096], BF16)
    b_xqf = reg.al([128, DC * 512], BF16)
    b_silu = [reg.al([128, 512], F32) for _ in range(2)]
    b_h = [reg.al([128, 512], F32) for _ in range(2)]
    b_hmx = reg.al([128, ICN], F32)

    E("sync", "dma_start", out=swg_sb[:], in_=swg_scr.rearrange("a p -> p a"),
      wait=[a2_done], inc=LD.inc(16), allow_slow_non_contiguous=True)
    E("sync", "dma_start", out=swu_sb[:], in_=swu_scr.rearrange("a p -> p a"),
      inc=LD.inc(16), allow_slow_non_contiguous=True)
    E("sync", "dma_start",
      out=wgT_sb[:].rearrange("p (n f) -> p n f", f=II),
      in_=wgT_scr.rearrange("n p f -> p n f"), inc=LD.inc(16))
    E("sync", "dma_start",
      out=wuT_sb[:].rearrange("p (n f) -> p n f", f=II),
      in_=wuT_scr.rearrange("n p f -> p n f"), inc=LD.inc(16))
    bw_ld = LD.at()
    E("vector", "scalar_tensor_tensor", scg_sb[:], swg_sb[:], 1.0,
      sx_b[:, 0:1].broadcast_to((128, ICN)), alu.bypass, alu.mult,
      wait=[bw_ld, x1_vv])
    E("vector", "scalar_tensor_tensor", scu_sb[:], swu_sb[:], 1.0,
      sx_b[:, 0:1].broadcast_to((128, ICN)), alu.bypass, alu.mult,
      inc=VV.inc(1))
    bsc_vv = VV.at()

    b_vmark = [None, None]
    b_simark = {}
    b_hmark = {}
    b_hstore = [None, None]
    NT_OWN = ROWS2 // 512

    for t in range(_lim("NT8B", NT8)):
        bi = t % 2
        own_tile = t < NT_OWN
        E("sync", "dma_start",
          out=b_x[bi][:].rearrange("p (a d) -> p a d", d=D),
          in_=x8t[4 * t:4 * t + 4].rearrange("a p d -> p a d"),
          wait=([b_vmark[bi]] if b_vmark[bi] else None), inc=LD.inc(16))
        ld = LD.at()
        E("vector", "nop", wait=[ld, bsc_vv])
        for j in range(4):
            col = 4 * t + j
            xx = b_x[bi][:, j * D:(j + 1) * D]
            E("vector", "tensor_scalar", xx, xx,
              rs_all[:, col:col + 1], mrs_all[:, col:col + 1],
              alu.mult, alu.subtract)
            E("vector", "tensor_scalar", xx, xx, MAGIC, MAGIC, alu.add,
              alu.subtract)
            E("vector", "tensor_scalar", b_xq[:, j * D:(j + 1) * D], xx,
              QMAX, -QMAX - 1.0, alu.min, alu.max)
        E("vector", "drain", inc=VV.inc(1))
        xq_done = VV.at()
        b_vmark[bi] = xq_done
        for dcp in range(4):
            for dc2 in range(2):
                dc = dcp * 2 + dc2
                for j in range(4):
                    tb = dc2 * 4 + j
                    E("tensor", "transpose",
                      bkb(2 + (dcp % 2))[:, tb * 128:tb * 128 + 128],
                      b_xq[:, j * D + dc * 128:j * D + dc * 128 + 128],
                      ident_b[:],
                      wait=([xq_done, VV.at()] if (dc2 == 0 and j == 0)
                            else None),
                      inc=(TT.inc(1) if (dc2 == 1 and j == 3) else None))
            E("vector", "tensor_copy",
              b_xqf[:, dcp * 1024:(dcp + 1) * 1024],
              bkb(2 + (dcp % 2))[:, 0:1024], wait=[TT.at()], inc=VV.inc(1))
        xqf_done = VV.at()
        for icp in range(ICN // 2):
            for sub in range(2):
                ic = 2 * icp + sub
                pp = sub
                wl_g = [xqf_done] if ic >= 2 else [xqf_done, bw_ld]
                if ic >= 2:
                    wl_g.append(b_simark[ic - 2])
                for dc in range(DC):
                    E("tensor", "matmul", bk(4 + pp),
                      wgT_sb[:, dc * II + ic * 128:dc * II + ic * 128 + 128],
                      b_xqf[:, dc * 512:(dc + 1) * 512],
                      start=(dc == 0), stop=(dc == DC - 1),
                      wait=(wl_g if dc == 0 else None))
                wl_u = [b_hmark[ic - 2]] if ic >= 2 else []
                for dc in range(DC):
                    E("tensor", "matmul", bk(6 + pp),
                      wuT_sb[:, dc * II + ic * 128:dc * II + ic * 128 + 128],
                      b_xqf[:, dc * 512:(dc + 1) * 512],
                      start=(dc == 0), stop=(dc == DC - 1),
                      wait=(wl_u if dc == 0 else None))
            E("tensor", "drain", inc=TT.inc(1))
            u_ps = TT.at()
            for sub in range(2):
                ic = 2 * icp + sub
                pp = sub
                wl_s = [u_ps]
                if ic >= 2:
                    wl_s.append(b_hmark[ic - 2])
                E("scalar", "activation", b_silu[pp][:], bk(4 + pp), AF.Silu,
                  scale=scg_sb[:, ic:ic + 1], wait=wl_s, inc=SS.inc(1))
                b_simark[ic] = SS.at()
                wl_h = [u_ps, SS.at()]
                if b_hstore[pp] is not None:
                    wl_h.append(b_hstore[pp])
                E("vector", "scalar_tensor_tensor", b_h[pp][:], bk(6 + pp),
                  scu_sb[:, ic:ic + 1], b_silu[pp][:], alu.mult, alu.mult,
                  wait=wl_h)
                E("vector", "reduce_max", b_hmx[:, ic:ic + 1], b_h[pp][:],
                  AX.X, apply_absolute_value=True, inc=VV.inc(1))
                b_hmark[ic] = VV.at()
                if own_tile:
                    E("gpsimd", "dma_start",
                      out=h_scr[ic, :, t * 512:(t + 1) * 512],
                      in_=b_h[pp][:], wait=[VV.at()], inc=GD.inc(16))
                    b_hstore[pp] = GD.at()
        E("vector", "tensor_max", hmax[:], hmax[:], b_hmx[:], inc=VV.inc(1))
        b_vmark[bi] = VV.at()
    E("vector", "reduce_max", gmax[:], hmax[:], AX.X, inc=VV.inc(1))
    fold_max()

    exchange_consume(pair_exchange(8, PL2, "x2"))
    scale_pipeline(1, sd_b, inv_sd_b)
    sd_ld = LD.at()
    if KDEBUG:
        E("sync", "dma_start", out=dbg["dbg_sc2"], in_=sc_x[0:1, 0:4],
          wait=[sd_ld], inc=ST.inc(16))

    if KDEBUG:
        E("vector", "tensor_copy", st_t1[0:1, 0:1], m_loc[0:1, 0:1])
        E("vector", "tensor_copy", st_t1[0:1, 1:2], m_glob[0:1, 0:1],
          inc=VV.inc(1))
        E("sync", "dma_start", out=dbg["dbg_m2"], in_=st_t1[0:1, 0:4],
          wait=[VV.at()], inc=ST.inc(16))

    reg = Region()
    wdT_sb = reg.al([128, ICN * 1024], BF16)
    c_h = [reg.al([128, ICN * 128], F32) for _ in range(2)]
    c_ih = [reg.al([128, ICN * 128], BF16) for _ in range(2)]
    c_p = [reg.al([128, 1024], F32) for _ in range(2)]
    c_o = [reg.al([128, 1024], F32) for _ in range(2)]
    swd_row = reg.al([1, 1024], F32)

    E("sync", "dma_start",
      out=wdT_sb[:].rearrange("p (n f) -> p n f", f=1024),
      in_=wdT_scr.rearrange("n p f -> p n f"),
      wait=[VV.at(), GD.at()], inc=LD.inc(16))
    E("sync", "dma_start",
      out=swd_row[0:1, :].rearrange("q (a p) -> q a p", p=128),
      in_=swd_scr.unsqueeze(0), inc=LD.inc(16))
    E("sync", "dma_start", out=swd_bc[0:1, :], in_=swd_row[0:1, :],
      wait=[LD.at()], inc=LD.inc(16))
    k = 1
    while k < 128:
        kk = min(k, 128 - k)
        E("sync", "dma_start", out=swd_bc[k:k + kk, :], in_=swd_bc[0:kk, :],
          wait=[LD.at()], inc=LD.inc(16))
        k *= 2
    cw_ld = LD.at()
    E("vector", "scalar_tensor_tensor", swd_bc[:], swd_bc[:], 1.0,
      sd_b[:, 0:1].broadcast_to((128, D)), alu.bypass, alu.mult,
      wait=[cw_ld, sd_ld], inc=VV.inc(1))
    cswd_vv = VV.at()

    y_t = y.rearrange("b (t p) d -> (b t) p d", p=128)
    NT_C = ROWS2 // 128
    c_vmark = [None, None]
    c_stmark = [None, None]
    c_tmark = [None, None]

    for t in range(_lim("NT_C", NT_C)):
        bi = t % 2
        wl = [c_vmark[bi]] if c_vmark[bi] else []
        if t == 0:
            wl.append(GD.at())
        E("sync", "dma_start",
          out=c_h[bi][:].rearrange("p (n f) -> p n f", f=128),
          in_=h_scr[:, :, t * 128:(t + 1) * 128].rearrange(
              "n p f -> p n f"),
          wait=wl, inc=LD.inc(16))
        E("sync", "dma_start", out=c_p[bi][:], in_=p_scr_t[t],
          inc=LD.inc(16))
        ld = LD.at()
        wlq = [ld, cswd_vv]
        if c_tmark[bi]:
            wlq.append(c_tmark[bi])
        E("vector", "scalar_tensor_tensor", c_h[bi][:], c_h[bi][:], 1.0,
          inv_sd_b[:, 0:1].broadcast_to((128, ICN * 128)), alu.bypass,
          alu.mult, wait=wlq)
        E("vector", "tensor_scalar", c_h[bi][:], c_h[bi][:], MAGIC, MAGIC,
          alu.add, alu.subtract)
        E("vector", "tensor_scalar", c_ih[bi][:], c_h[bi][:], QMAX,
          -QMAX - 1.0, alu.min, alu.max, inc=VV.inc(1))
        ih_done = VV.at()
        for hh in range(2):
            for ic in range(ICN):
                E("tensor", "matmul", bk(hh),
                  c_ih[bi][:, ic * 128:(ic + 1) * 128],
                  wdT_sb[:, ic * 1024 + hh * 512:ic * 1024 + (hh + 1) * 512],
                  start=(ic == 0), stop=(ic == ICN - 1),
                  wait=([ih_done, cw_ld] if (hh == 0 and ic == 0) else None))
        E("tensor", "drain", inc=TT.inc(1))
        dn_ps = TT.at()
        c_tmark[bi] = dn_ps
        wl2 = [dn_ps]
        if c_stmark[bi]:
            wl2.append(c_stmark[bi])
        E("vector", "scalar_tensor_tensor", c_o[bi][:, 0:512], bk(0), 1.0,
          swd_bc[:, 0:512], alu.bypass, alu.mult, wait=wl2)
        E("vector", "scalar_tensor_tensor", c_o[bi][:, 512:1024], bk(1),
          1.0, swd_bc[:, 512:1024], alu.bypass, alu.mult)
        E("vector", "tensor_add", c_o[bi][:], c_o[bi][:], c_p[bi][:],
          inc=VV.inc(1))
        c_vmark[bi] = VV.at()
        E("sync", "dma_start", out=y_t[t], in_=c_o[bi][:], wait=[VV.at()],
          inc=ST.inc(16))
        c_stmark[bi] = ST.at()

    if KDEBUG:
        E("sync", "dma_start", out=dbg["dbg_a0"], in_=a_scr_t[0],
          wait=[ST.at()], inc=ST.inc(16))
        E("sync", "dma_start", out=dbg["dbg_a31"], in_=a_scr_t[31],
          inc=ST.inc(16))
        E("sync", "dma_start", out=dbg["dbg_p0"], in_=p_scr_t[0],
          inc=ST.inc(16))
        E("sync", "dma_start", out=dbg["dbg_h0"], in_=h_scr[:, :, 0:128],
          inc=ST.inc(16))
        E("sync", "dma_start", out=dbg["dbg_g0"], in_=g_scr[:, :, 0:128],
          inc=ST.inc(16))

    fin = ST.at()
    streams["sync"].append(lambda e, _f=fin: e.wait_ge(*_f))

    with nc.Block() as block:
        @block.sync
        def _(e):
            for f in streams["sync"]:
                f(e)

        @block.vector
        def _(e):
            for f in streams["vector"]:
                f(e)

        @block.scalar
        def _(e):
            for f in streams["scalar"]:
                f(e)

        @block.tensor
        def _(e):
            for f in streams["tensor"]:
                f(e)

        @block.gpsimd
        def _(e):
            for f in streams["gpsimd"]:
                f(e)

    return nc


_NC_CACHE = None


def kernel(L0_attn_x, L0_attn_wq, L0_attn_wk, L0_attn_wv, L0_attn_wo,
           L0_attn_proj, L0_attn_w1, L0_attn_b1, L0_attn_w2, L0_attn_b2,
           L0_attn_g1, L0_attn_beta1, L0_attn_g2, L0_attn_beta2,
           L0_ffn_clip_x, L0_ffn_clip_down, L0_ffn_w_gate, L0_ffn_w_up,
           L0_ffn_w_down):
    global _NC_CACHE
    if _NC_CACHE is None:
        _NC_CACHE = build_nc()
    nc = _NC_CACHE

    x = np.ascontiguousarray(np.asarray(L0_attn_x, np.float32))
    f32 = lambda a: np.ascontiguousarray(np.asarray(a, np.float32))
    com = dict(
        wq=f32(L0_attn_wq), wk=f32(L0_attn_wk), wv=f32(L0_attn_wv),
        wo=f32(L0_attn_wo), proj=f32(L0_attn_proj), w1=f32(L0_attn_w1),
        w2=f32(L0_attn_w2), wg=f32(L0_ffn_w_gate), wu=f32(L0_ffn_w_up),
        wd=f32(L0_ffn_w_down),
        g1T=np.ascontiguousarray(f32(L0_attn_g1).reshape(DC, 128).T),
        beta1T=np.ascontiguousarray(f32(L0_attn_beta1).reshape(DC, 128).T),
        g2T=np.ascontiguousarray(f32(L0_attn_g2).reshape(DC, 128).T),
        beta2T=np.ascontiguousarray(f32(L0_attn_beta2).reshape(DC, 128).T),
        b1row=f32(L0_attn_b1).reshape(1, FF),
        b2row=f32(L0_attn_b2).reshape(1, D),
        clip_x=f32(L0_ffn_clip_x).reshape(1, 1),
        clip_dn=f32(L0_ffn_clip_down).reshape(1, 1),
    )
    nz = np.frombuffer(os.urandom(8), dtype=np.int32).reshape(1, 2).copy()
    nz |= 1
    in_maps = []
    for c in range(NC):
        sb = _stat_batches(c)
        in_maps.append(dict(com, x8=np.ascontiguousarray(x[sb]), nonce=nz))
    res = run_bass_kernel_spmd(nc, in_maps, core_ids=list(range(NC)))
    out = np.empty((B, S, D), np.float32)
    for c in range(NC):
        out[2 * c:2 * c + 2] = res.results[c]["y"]
    return out


# version 11
# speedup vs baseline: 1.0304x; 1.0304x over previous
import os
import numpy as np
import concourse.bass as bass
import concourse.mybir as mybir
from concourse.bass_utils import run_bass_kernel_spmd
from concourse.alu_op_type import AluOpType as alu

F32 = mybir.dt.float32
BF16 = mybir.dt.bfloat16
I32 = mybir.dt.int32
AF = mybir.ActivationFunctionType
AX = mybir.AxisListType

B, S, D, H, DH, FF, II = 16, 4096, 1024, 16, 64, 4096, 2816
NC = 8
SBATCH = 8
OB = 2
MAGIC = 12582912.0
QMAX = 127.0
EPS = 1e-5
ROWS8 = SBATCH * S
NT8 = ROWS8 // 512
ROWS2 = OB * S
DC = D // 128
ICN = II // 128
FC1 = FF // 128

KDEBUG = False
LIM = {}


def _lim(name, full):
    return LIM.get(name, full)


def _stat_batches(c):
    p = c // 2
    own = [2 * c, 2 * c + 1]
    pool = [b for b in range(B) if not (4 * p <= b < 4 * p + 4)]
    extra = pool[0:6] if c % 2 == 0 else pool[6:12]
    return own + extra


def build_nc():
    nc = bass.Bass("TRN2", target_bir_lowering=False,
                   detect_race_conditions=False)

    def din(name, shape, dt=F32):
        return nc.dram_tensor(name, shape, dt, kind="ExternalInput").ap()

    x8 = din("x8", [SBATCH, S, D])
    wq = din("wq", [D, D]); wk = din("wk", [D, D]); wv = din("wv", [D, D])
    wo = din("wo", [D, D]); proj = din("proj", [DH, DH])
    w1 = din("w1", [D, FF]); w2 = din("w2", [FF, D])
    wg = din("wg", [II, D]); wu = din("wu", [II, D]); wd = din("wd", [D, II])
    g1T = din("g1T", [128, DC]); beta1T = din("beta1T", [128, DC])
    g2T = din("g2T", [128, DC]); beta2T = din("beta2T", [128, DC])
    b1row_i = din("b1row", [1, FF]); b2row_i = din("b2row", [1, D])
    clip_x = din("clip_x", [1, 1]); clip_dn = din("clip_dn", [1, 1])
    nonce = din("nonce", [1, 2], I32)

    y = nc.dram_tensor("y", [OB, S, D], F32, kind="ExternalOutput").ap()
    dbg = {}
    if KDEBUG:
        for nm, shp in [("dbg_stats_m", [128, 256]), ("dbg_stats_r", [128, 256]),
                        ("dbg_m1", [1, 4]), ("dbg_m2", [1, 4]),
                        ("dbg_sc1", [1, 4]), ("dbg_sc2", [1, 4]),
                        ("dbg_a0", [128, 1024]), ("dbg_a31", [128, 1024]),
                        ("dbg_p0", [128, 1024]), ("dbg_h0", [22, 128, 128])]:
            dbg[nm] = nc.dram_tensor(nm, shp, F32, kind="ExternalOutput").ap()
        dbg["dbg_g0"] = nc.dram_tensor("dbg_g0", [32, 128, 128], BF16,
                                       kind="ExternalOutput").ap()

    a_scr = nc.dram_tensor("a_scr", [ROWS2, D], F32, kind="Internal").ap()
    p_scr = nc.dram_tensor("p_scr", [ROWS2, D], F32, kind="Internal").ap()
    h_scr = nc.dram_tensor("h_scr", [ICN, 128, ROWS2], F32, kind="Internal").ap()
    wgT_scr = nc.dram_tensor("wgT_scr", [DC, 128, II], BF16, kind="Internal").ap()
    wuT_scr = nc.dram_tensor("wuT_scr", [DC, 128, II], BF16, kind="Internal").ap()
    wdT_scr = nc.dram_tensor("wdT_scr", [ICN, 128, D], BF16, kind="Internal").ap()
    swg_scr = nc.dram_tensor("swg_scr", [ICN, 128], F32, kind="Internal").ap()
    swu_scr = nc.dram_tensor("swu_scr", [ICN, 128], F32, kind="Internal").ap()
    swd_scr = nc.dram_tensor("swd_scr", [DC, 128], F32, kind="Internal").ap()
    g_scr = nc.dram_tensor("g_scr", [FC1, 128, ROWS2], BF16,
                           kind="Internal").ap()
    mail = nc.dram_tensor("mail", [1, 16], I32, kind="Internal",
                          addr_space="Shared").ap()

    A = nc.alloc_sbuf_tensor
    stats_m = A("stats_m", [128, 256], F32)
    stats_r = A("stats_r", [128, 256], F32)
    g1_sb = A("g1_sb", [128, DC], F32)
    g2_sb = A("g2_sb", [128, DC], F32)
    b1f_b = A("b1f_b", [128, DC], BF16)
    b2f_b = A("b2f_b", [128, DC], BF16)
    bq_row = A("bq_row", [1, D], BF16)
    bk_row = A("bk_row", [1, D], BF16)
    bv_row = A("bv_row", [1, D], BF16)
    b1p_row = A("b1p_row", [1, FF], BF16)
    b2_bc = A("b2_bc", [128, D], F32)
    swd_bc = A("swd_bc", [128, D], F32)
    onerow = A("onerow", [1, 512], BF16)
    tri_mask = A("tri_mask", [128, 128], F32)
    ident_b = A("ident_b", [128, 128], BF16)
    proj2 = A("proj2", [128, 128], BF16)
    S_sb = A("S_sb", [128, 16 * 65], F32)
    S_b = A("S_b", [128, 16 * 65], BF16)
    V_aug = A("V_aug", [128, 16 * 65], BF16)
    num_sb = A("num_sb", [128, 2 * 520], F32)
    den_sb = A("den_sb", [128, 16], F32)
    rec_sb = A("rec_sb", [128, 16], F32)
    sq_waste = A("sq_waste", [128, 1024], F32)
    st_sum = A("st_sum", [128, 4], F32)
    st_sq = A("st_sq", [128, 4], F32)
    st_mx = A("st_mx", [128, 4], F32)
    st_mn = A("st_mn", [128, 4], F32)
    st_t1 = A("st_t1", [128, 4], F32)
    st_t2 = A("st_t2", [128, 4], F32)
    gmax = A("gmax", [128, 1], F32)
    hmax = A("hmax", [128, ICN], F32)
    fold32 = A("fold32", [1, 128], F32)
    m_loc = A("m_loc", [1, 1], F32)
    m_glob = A("m_glob", [1, 1], F32)
    sc_x = A("sc_x", [1, 4], F32)
    inv_sx_b = A("inv_sx_b", [128, 1], F32)
    sx_b = A("sx_b", [128, 1], F32)
    inv_sd_b = A("inv_sd_b", [128, 1], F32)
    sd_b = A("sd_b", [128, 1], F32)
    rs_all = A("rs_all", [128, 256], F32)
    mrs_all = A("mrs_all", [128, 256], F32)
    nmrs_all = A("nmrs_all", [128, 256], F32)
    swg_sb = A("swg_sb", [128, ICN], F32)
    swu_sb = A("swu_sb", [128, ICN], F32)
    scg_sb = A("scg_sb", [128, ICN], F32)
    scu_sb = A("scu_sb", [128, ICN], F32)
    sendb = A("sendb", [1, 4], I32)
    pollb = A("pollb", [1, 4], I32)
    nonce_sb = A("nonce_sb", [1, 2], I32)
    clip_sb = A("clip_sb", [1, 2], F32)
    a1st = A("a1st", [128, 4], F32)

    SCRATCH = 160 * 1024
    bret = nc.bump_sbuf(SCRATCH)
    assert bret is not None, "sbuf scratch reservation failed"
    scratch_base = bret[0] if isinstance(bret, tuple) else bret

    class Region:
        cnt = 0

        def __init__(self):
            self.off = scratch_base

        def al(self, shape, dt):
            bpp = int(np.prod(shape[1:])) * mybir.dt.size(dt)
            off = (self.off + 31) & ~31
            Region.cnt += 1
            t = nc.alloc_sbuf_tensor_at(f"scr{Region.cnt}", shape, dt,
                                        offset=off)
            self.off = off + bpp
            assert self.off <= scratch_base + SCRATCH, "scratch overflow"
            return t

    P8 = nc.alloc_psum_tensor("P8", [128, 4096], F32)
    P8b = P8.bitcast(BF16)

    def bk(i):
        return P8[:, i * 512:(i + 1) * 512]

    def bkb(i):
        return P8b[:, i * 1024:(i + 1) * 1024]

    def banks_head(nn):
        return P8[:].rearrange("p (b c) -> p b c", c=512)[:, :, 0:nn]

    streams = {k: [] for k in ("sync", "vector", "scalar", "tensor", "gpsimd")}

    class Sem:
        def __init__(self, name):
            self.h = nc.alloc_semaphore(name)
            self.v = 0

        def inc(self, n):
            self.v += n
            return (self.h, n)

        def at(self):
            return (self.h, self.v)

    LD = Sem("LD"); GD = Sem("GD"); TT = Sem("TT"); VV = Sem("VV")
    SS = Sem("SS"); GG = Sem("GG"); ST = Sem("ST")
    PL1 = nc.alloc_semaphore("PL1")
    PL2 = nc.alloc_semaphore("PL2")

    def _hz(ap):
        shape = getattr(ap, "shape", None)
        if shape is None or not hasattr(ap, "stride"):
            return False
        f = 1
        for s in shape[1:]:
            f *= s
        if f < 128:
            return True
        try:
            if ap.stride() == 0:
                return True
        except Exception:
            pass
        return False

    def E(eng, meth, *a, inc=None, wait=None, nodr=False, **kw):
        dr = (not nodr and meth not in ("dma_start", "drain", "nop", "memset")
              and eng in ("vector", "scalar", "gpsimd")
              and (any(_hz(x) for x in a[1:])
                   or any(_hz(v) for v in kw.values())))

        def f(e, _m=meth, _a=a, _k=kw, _i=inc, _w=wait, _d=dr):
            if _w:
                for sh, sv in _w:
                    e.wait_ge(sh, sv)
            if _d:
                e.drain()
            ins = getattr(e, _m)(*_a, **_k)
            if _i:
                ins.then_inc(*_i)
        streams[eng].append(f)

    E("sync", "dma_start", out=g1_sb[:], in_=g1T, inc=LD.inc(16))
    E("sync", "dma_start", out=g2_sb[:], in_=g2T, inc=LD.inc(16))
    E("sync", "dma_start", out=nonce_sb[:], in_=nonce, inc=LD.inc(16))
    E("sync", "dma_start", out=clip_sb[0:1, 0:1], in_=clip_x, inc=LD.inc(16))
    E("sync", "dma_start", out=clip_sb[0:1, 1:2], in_=clip_dn, inc=LD.inc(16))
    E("sync", "dma_start", out=b2_bc[0:1, :], in_=b2row_i, inc=LD.inc(16))
    setup_ld = LD.at()
    k = 1
    while k < 128:
        kk = min(k, 128 - k)
        E("sync", "dma_start", out=b2_bc[k:k + kk, :], in_=b2_bc[0:kk, :],
          wait=[LD.at()], inc=LD.inc(16))
        k *= 2
    b2bc_ld = LD.at()
    E("gpsimd", "memset", tri_mask[:], 1.0)
    E("gpsimd", "affine_select", tri_mask[:], tri_mask[:], [[1, 128]],
      alu.is_ge, 0.0, base=0, channel_multiplier=-1)
    E("gpsimd", "memset", ident_b[:], 1.0)
    E("gpsimd", "affine_select", ident_b[:], ident_b[:], [[1, 128]],
      alu.is_equal, 0.0, base=0, channel_multiplier=-1)
    E("gpsimd", "memset", onerow[:], 1.0)
    E("gpsimd", "memset", gmax[:], 0.0)
    E("gpsimd", "memset", stats_m[:], 0.0)
    E("gpsimd", "memset", stats_r[:], 1.0)
    E("gpsimd", "memset", hmax[:], 0.0)
    E("gpsimd", "memset", proj2[:], 0.0)
    vaug_v = V_aug[:].rearrange("p (h c) -> p h c", h=16)
    E("gpsimd", "memset", vaug_v[:, :, 64:65], 1.0, inc=GG.inc(1))
    E("gpsimd", "dma_start", out=b1f_b[:], in_=beta1T, inc=GD.inc(16))
    E("gpsimd", "dma_start", out=b2f_b[:], in_=beta2T, inc=GD.inc(16))
    E("gpsimd", "dma_start", out=proj2[0:64, 0:64], in_=proj,
      wait=[GG.at()], inc=GD.inc(16))
    proj_gd = GD.at()
    E("vector", "tensor_copy", proj2[64:128, 64:128], proj2[0:64, 0:64],
      wait=[proj_gd], inc=VV.inc(1))

    reg = Region()
    wp_f = [reg.al([128, II], F32) for _ in range(2)]
    wp_i = [reg.al([128, II], BF16) for _ in range(2)]
    wp_t = [reg.al([128, ICN * 128], BF16) for _ in range(2)]
    wp_mx = reg.al([128, 2], F32)
    wp_sc = reg.al([128, 2], F32)

    def prep_weight(w_ap, n_chunks, wT_scr_ap, s_scr_ap, free_len):
        w_t = w_ap.rearrange("(n p) f -> n p f", p=128)
        ntr = free_len // 128
        for ch in range(n_chunks):
            bi = ch % 2
            E("sync", "dma_start", out=wp_f[bi][:, 0:free_len],
              in_=w_t[ch], wait=[VV.at(), GD.at()], inc=LD.inc(16))
            ld = LD.at()
            E("vector", "reduce_max", wp_mx[:, bi:bi + 1],
              wp_f[bi][:, 0:free_len], AX.X, apply_absolute_value=True,
              wait=[ld])
            E("vector", "reciprocal", wp_sc[:, bi:bi + 1], wp_mx[:, bi:bi + 1])
            E("vector", "tensor_scalar", wp_sc[:, bi:bi + 1],
              wp_sc[:, bi:bi + 1], QMAX, None, alu.mult)
            E("vector", "tensor_scalar", wp_mx[:, bi:bi + 1],
              wp_mx[:, bi:bi + 1], 1.0 / QMAX, None, alu.mult, inc=VV.inc(1))
            E("gpsimd", "dma_start",
              out=s_scr_ap[ch:ch + 1, :].rearrange("a p -> p a"),
              in_=wp_mx[:, bi:bi + 1], wait=[VV.at()], inc=GD.inc(16),
              allow_slow_non_contiguous=True)
            E("vector", "scalar_tensor_tensor", wp_f[bi][:, 0:free_len],
              wp_f[bi][:, 0:free_len], 1.0,
              wp_sc[:, bi:bi + 1].broadcast_to((128, free_len)),
              alu.bypass, alu.mult)
            E("vector", "tensor_scalar", wp_f[bi][:, 0:free_len],
              wp_f[bi][:, 0:free_len], MAGIC, MAGIC, alu.add, alu.subtract)
            E("vector", "tensor_scalar", wp_i[bi][:, 0:free_len],
              wp_f[bi][:, 0:free_len], QMAX, -QMAX - 1.0, alu.min, alu.max,
              inc=VV.inc(1))
            vq = VV.at()
            pbank = 4 * bi
            for tb in range(ntr):
                g8 = tb // 8
                bb = pbank + (g8 % 2)
                E("tensor", "transpose",
                  bkb(bb)[:, (tb % 8) * 128:(tb % 8) * 128 + 128],
                  wp_i[bi][:, tb * 128:tb * 128 + 128], ident_b[:],
                  wait=([vq, SS.at()] if tb % 8 == 0 else None),
                  inc=(TT.inc(1) if (tb % 8 == 7 or tb == ntr - 1) else None))
                if tb % 8 == 7 or tb == ntr - 1:
                    lo = g8 * 8
                    n = tb - lo + 1
                    E("scalar", "copy", wp_t[bi][:, lo * 128:(lo + n) * 128],
                      bkb(bb)[:, 0:n * 128], wait=[TT.at()], inc=SS.inc(1))
            E("gpsimd", "dma_start",
              out=wT_scr_ap[0:ntr, :, ch * 128:ch * 128 + 128].rearrange(
                  "n p f -> p n f"),
              in_=wp_t[bi][:, 0:ntr * 128].rearrange("p (n f) -> p n f",
                                                     f=128),
              wait=[SS.at()], inc=GD.inc(16))

    prep_weight(wg, ICN, wgT_scr, swg_scr, D)
    prep_weight(wu, ICN, wuT_scr, swu_scr, D)
    prep_weight(wd, DC, wdT_scr, swd_scr, II)
    prep_gd = GD.at()
    prep_vv = VV.at()
    prep_ss = SS.at()
    prep_tt = TT.at()

    x8t = x8.rearrange("b (t p) d -> (b t) p d", p=128)
    a0x = [reg.al([128, 4096], F32) for _ in range(2)]
    a0_vmark = [None, None]

    for t in range(_lim("NT8", NT8)):
        bi = t % 2
        E("sync", "dma_start",
          out=a0x[bi][:].rearrange("p (a d) -> p a d", d=D),
          in_=x8t[4 * t:4 * t + 4].rearrange("a p d -> p a d"),
          wait=([a0_vmark[bi]] if a0_vmark[bi] else None), inc=LD.inc(16))
        ld = LD.at()
        E("scalar", "nop", wait=[ld, VV.at()])
        E("vector", "nop", wait=[ld])
        for j in range(4):
            xx = a0x[bi][:, j * D:(j + 1) * D]
            E("scalar", "activation", sq_waste[:], xx, AF.Square,
              accum_out=st_sq[:, j:j + 1])
        for j in range(4):
            xx = a0x[bi][:, j * D:(j + 1) * D]
            E("scalar", "activation", sq_waste[:], xx, AF.Identity,
              accum_out=st_sum[:, j:j + 1])
            E("vector", "reduce_max", st_mx[:, j:j + 1], xx, AX.X)
            E("vector", "tensor_reduce", st_mn[:, j:j + 1], xx, AX.X, alu.min)
        E("scalar", "drain", inc=SS.inc(1))
        sq_done = SS.at()
        cols = slice(4 * t, 4 * t + 4)
        E("vector", "tensor_scalar", stats_m[:, cols], st_sum[:], 1.0 / D,
          None, alu.mult, wait=[sq_done])
        E("vector", "tensor_scalar", st_sq[:], st_sq[:], 1.0 / D, None,
          alu.mult)
        E("vector", "tensor_mul", st_t1[:], stats_m[:, cols], stats_m[:, cols])
        E("vector", "tensor_sub", st_sq[:], st_sq[:], st_t1[:])
        E("vector", "tensor_scalar", st_sq[:], st_sq[:], EPS, None, alu.add,
          inc=VV.inc(1))
        E("scalar", "activation", st_sq[:], st_sq[:], AF.Sqrt,
          wait=[VV.at()], inc=SS.inc(1))
        E("vector", "reciprocal", stats_r[:, cols], st_sq[:], wait=[SS.at()])
        E("vector", "tensor_sub", st_t1[:], st_mx[:], stats_m[:, cols])
        E("vector", "tensor_sub", st_t2[:], stats_m[:, cols], st_mn[:])
        E("vector", "tensor_max", st_t1[:], st_t1[:], st_t2[:])
        E("vector", "tensor_mul", st_t1[:], st_t1[:], stats_r[:, cols])
        E("vector", "reduce_max", st_t2[:, 0:1], st_t1[:], AX.X)
        E("vector", "tensor_max", gmax[:], gmax[:], st_t2[:, 0:1],
          inc=VV.inc(1))
        a0_vmark[bi] = VV.at()

    def fold_max():
        E("sync", "dma_start", out=fold32[:], in_=gmax[:, 0:1],
          wait=[VV.at()], inc=LD.inc(16))
        E("vector", "reduce_max", m_loc[:], fold32[:], AX.X,
          wait=[LD.at()], inc=VV.inc(1))

    def pair_exchange(slot_base, poll_sem, tag):
        E("vector", "tensor_copy", sendb[0:1, 0:2], nonce_sb[0:1, 0:2],
          wait=[setup_ld])
        E("vector", "tensor_copy", sendb[0:1, 2:3],
          m_loc[0:1, 0:1].bitcast(I32))
        E("vector", "memset", sendb[0:1, 3:4], 0, inc=VV.inc(1))
        vready = VV.at()

        def g_exchange(g, _vr=vready, _sb=slot_base, _ps=poll_sem, _t=tag):
            g.wait_ge(*_vr)
            cid = g.partition_id()
            with (g.register(f"par{_t}") as rpar,
                  g.register(f"rnot{_t}") as rnot,
                  g.register(f"rcnt{_t}") as rcnt,
                  g.register(f"rn0{_t}") as rn0,
                  g.register(f"rn1{_t}") as rn1,
                  g.register(f"rf0{_t}") as rf0,
                  g.register(f"rf1{_t}") as rf1,
                  g.register(f"rd0{_t}") as rd0,
                  g.register(f"rd1{_t}") as rd1):
                g.reg_mov(rpar, cid % 2)
                g.reg_load(rn0, nonce_sb[0:1, 0:1])
                g.reg_load(rn1, nonce_sb[0:1, 1:2])
                g.reg_mov(rnot, 1)
                g.reg_mov(rcnt, 0)
                for par in (0, 1):
                    my = _sb + 4 * par
                    ot = _sb + 4 * (1 - par)
                    with g.If_eq(rpar, par):
                        g.dma_start(out=mail[0:1, my + 2:my + 4],
                                    in_=sendb[0:1, 2:4]).then_inc(_ps, 16)
                        g.reg_add(rcnt, rcnt, 16)
                        g.wait_ge(_ps, rcnt)
                        g.dma_start(out=mail[0:1, my:my + 2],
                                    in_=sendb[0:1, 0:2]).then_inc(_ps, 16)
                        g.reg_add(rcnt, rcnt, 16)
                        g.wait_ge(_ps, rcnt)
                        with g.While(rnot):
                            g.reg_add(rcnt, rcnt, 16)
                            g.dma_start(out=pollb[0:1, 0:4],
                                        in_=mail[0:1, ot:ot + 4]
                                        ).then_inc(_ps, 16)
                            g.wait_ge(_ps, rcnt)
                            g.reg_load(rf0, pollb[0:1, 0:1])
                            g.reg_load(rf1, pollb[0:1, 1:2])
                            g.reg_alu(rd0, rf0, rn0, alu.bitwise_xor)
                            g.reg_alu(rd1, rf1, rn1, alu.bitwise_xor)
                            g.reg_alu(rnot, rd0, rd1, alu.bitwise_or)
        streams["gpsimd"].append(g_exchange)
        E("gpsimd", "drain", inc=GG.inc(1))
        return GG.at()

    def exchange_consume(gg_mark):
        E("vector", "tensor_tensor", m_glob[0:1, 0:1], m_loc[0:1, 0:1],
          pollb[0:1, 2:3].bitcast(F32), alu.max, wait=[gg_mark],
          inc=VV.inc(1))

    def scale_pipeline(clip_col, s_bc, inv_bc):
        E("scalar", "activation", sc_x[0:1, 0:1],
          clip_sb[0:1, clip_col:clip_col + 1], AF.Sigmoid,
          wait=[VV.at()], inc=SS.inc(1))
        E("vector", "tensor_mul", sc_x[0:1, 1:2], sc_x[0:1, 0:1],
          m_glob[0:1, 0:1], wait=[SS.at()])
        E("vector", "tensor_scalar", sc_x[0:1, 1:2], sc_x[0:1, 1:2],
          1.0 / QMAX, None, alu.mult)
        E("vector", "reciprocal", sc_x[0:1, 2:3], sc_x[0:1, 1:2],
          inc=VV.inc(1))
        E("sync", "dma_start", out=s_bc[0:1, :], in_=sc_x[0:1, 1:2],
          wait=[VV.at()], inc=LD.inc(16))
        E("sync", "dma_start", out=inv_bc[0:1, :], in_=sc_x[0:1, 2:3],
          inc=LD.inc(16))
        k2 = 1
        while k2 < 128:
            kk2 = min(k2, 128 - k2)
            E("sync", "dma_start", out=s_bc[k2:k2 + kk2, :],
              in_=s_bc[0:kk2, :], wait=[LD.at()], inc=LD.inc(16))
            E("sync", "dma_start", out=inv_bc[k2:k2 + kk2, :],
              in_=inv_bc[0:kk2, :], inc=LD.inc(16))
            k2 *= 2

    fold_max()
    x1_gg = pair_exchange(0, PL1, "x1")

    reg = Region()
    wq_sb = reg.al([128, DC * 1024], BF16)
    wk_sb = reg.al([128, DC * 1024], BF16)
    wv_sb = reg.al([128, DC * 1024], BF16)
    wo_sb = reg.al([128, DC * 1024], BF16)
    a1_stage = [reg.al([128, 1024], F32) for _ in range(2)]
    x_t = reg.al([128, 1024], F32)
    xn_t = reg.al([128, 1024], F32)
    a_t = reg.al([128, 1024], F32)
    h1_t = reg.al([128, 1024], BF16)
    h1f_t = reg.al([128, 1024], BF16)
    qt_sb = reg.al([128, 1024], BF16)
    kt_sb = reg.al([128, 1024], BF16)
    Q_sb = reg.al([128, 1024], BF16)
    K_sb = reg.al([128, 1024], BF16)
    Ktm_sb = reg.al([128, 1024], BF16)
    A_sb = reg.al([128, 2048], BF16)
    at_fm = reg.al([128, 1024], BF16)
    atn_b = reg.al([128, 1024], BF16)
    phi_e = reg.al([128, 1024], F32)
    phi_r = reg.al([128, 1024], F32)
    phi_k_e = reg.al([128, 1024], F32)
    phi_k_r = reg.al([128, 1024], F32)
    brow_f = reg.al([1, 1024], F32)

    def load_w_and_bias(w_ap, dst_sb, brow_b, gcol, first):
        wt = w_ap.rearrange("(n p) f -> n p f", p=128)
        for ch in range(DC):
            bi = ch % 2
            wl = [VV.at()]
            if first and ch == 0:
                wl += [prep_gd, prep_vv, prep_ss, prep_tt, b2bc_ld]
            E("sync", "dma_start", out=a1_stage[bi][:], in_=wt[ch],
              wait=wl, inc=LD.inc(16))
            E("vector", "tensor_copy", dst_sb[:, ch * 1024:(ch + 1) * 1024],
              a1_stage[bi][:], wait=[LD.at()], inc=VV.inc(1))
        wcast = VV.at()
        for hh in range(2):
            for ch in range(DC):
                E("tensor", "matmul", bk(hh)[0:1, :],
                  b1f_b[:, ch:ch + 1],
                  dst_sb[:, ch * 1024 + hh * 512:ch * 1024 + (hh + 1) * 512],
                  start=(ch == 0), stop=(ch == DC - 1),
                  wait=([wcast, VV.at()] if ch == 0 else None),
                  inc=(TT.inc(1) if ch == DC - 1 else None))
        E("vector", "tensor_copy", brow_f[0:1, 0:512], bk(0)[0:1, :],
          wait=[TT.at()])
        E("vector", "tensor_copy", brow_f[0:1, 512:1024], bk(1)[0:1, :])
        E("vector", "tensor_copy", brow_b[0:1, :], brow_f[0:1, :],
          inc=VV.inc(1))
        for ch in range(DC):
            E("vector", "tensor_scalar", dst_sb[:, ch * 1024:(ch + 1) * 1024],
              dst_sb[:, ch * 1024:(ch + 1) * 1024], gcol[:, ch:ch + 1],
              None, alu.mult)
        E("vector", "drain", inc=VV.inc(1))

    load_w_and_bias(wq, wq_sb, bq_row, g1_sb, True)
    load_w_and_bias(wk, wk_sb, bk_row, g1_sb, False)
    load_w_and_bias(wv, wv_sb, bv_row, g1_sb, False)
    a1w_vv = VV.at()
    E("gpsimd", "dma_start",
      out=wo_sb[:].rearrange("p (n f) -> p n f", f=1024),
      in_=wo.rearrange("(n p) f -> p n f", p=128), wait=[a1w_vv],
      inc=GD.inc(16))
    a1_wo_gd = GD.at()

    x2t = x8.rearrange("b (t p) d -> (b t) p d", p=128)
    a_scr_t = a_scr.rearrange("(t p) d -> t p d", p=128)
    NCHUNK = S // 128
    astore = {"m": None}

    def a1_chunk(gtile):
        sc = gtile
        E("sync", "dma_start", out=x_t[:], in_=x2t[gtile],
          wait=[VV.at()], inc=LD.inc(16))
        ld = LD.at()
        E("vector", "tensor_scalar", xn_t[:], x_t[:],
          stats_m[:, sc:sc + 1], stats_r[:, sc:sc + 1],
          alu.subtract, alu.mult, wait=[ld], inc=VV.inc(1))
        xn_done = VV.at()
        E("scalar", "activation", sq_waste[:], xn_t[:], AF.Square,
          accum_out=a1st[:, 1:2], wait=[xn_done], inc=SS.inc(1))
        E("vector", "reduce_sum", a1st[:, 0:1], xn_t[:], AX.X)
        E("vector", "tensor_scalar", a1st[:, 0:1], a1st[:, 0:1], 1.0 / D,
          None, alu.mult)
        E("vector", "tensor_scalar", a1st[:, 1:2], a1st[:, 1:2], 1.0 / D,
          None, alu.mult, wait=[SS.at()])
        E("vector", "tensor_mul", a1st[:, 2:3], a1st[:, 0:1], a1st[:, 0:1])
        E("vector", "tensor_sub", a1st[:, 1:2], a1st[:, 1:2], a1st[:, 2:3])
        E("vector", "tensor_scalar", a1st[:, 1:2], a1st[:, 1:2], EPS, None,
          alu.add, inc=VV.inc(1))
        E("scalar", "activation", a1st[:, 1:2], a1st[:, 1:2], AF.Sqrt,
          wait=[VV.at()], inc=SS.inc(1))
        E("vector", "reciprocal", a1st[:, 3:4], a1st[:, 1:2], wait=[SS.at()])
        E("vector", "tensor_scalar", h1_t[:], xn_t[:], a1st[:, 0:1],
          a1st[:, 3:4], alu.subtract, alu.mult, inc=VV.inc(1))
        h1_done = VV.at()
        if _lim("A1W", 99) < 1:
            return
        for fc in range(DC):
            E("tensor", "transpose", bkb(0)[:, fc * 128:fc * 128 + 128],
              h1_t[:, fc * 128:fc * 128 + 128], ident_b[:],
              wait=([h1_done, SS.at()] if fc == 0 else None),
              inc=(TT.inc(1) if fc == DC - 1 else None))
        E("scalar", "copy", h1f_t[:], bkb(0)[:, 0:1024], wait=[TT.at()],
          inc=SS.inc(1))
        h1f_done = SS.at()
        if _lim("A1W", 99) < 2:
            return
        for fc in range(DC):
            qcol = 1024 + fc * 128
            for dc in range(DC):
                E("tensor", "matmul", P8[:, qcol:qcol + 128],
                  wq_sb[:, dc * 1024 + fc * 128:dc * 1024 + fc * 128 + 128],
                  h1f_t[:, dc * 128:dc * 128 + 128],
                  start=(dc == 0), stop=False,
                  wait=([h1f_done] if (fc == 0 and dc == 0) else None))
            E("tensor", "matmul", P8[:, qcol:qcol + 128],
              bq_row[0:1, fc * 128:fc * 128 + 128], onerow[0:1, 0:128],
              start=False, stop=True,
              inc=(TT.inc(1) if fc == DC - 1 else None))
        qwave = TT.at()
        E("scalar", "copy", qt_sb[:], P8[:, 1024:2048],
          wait=[qwave], inc=SS.inc(1))
        q_evac = SS.at()
        if _lim("A1W", 99) < 3:
            return
        for fc in range(DC):
            kcol = 2048 + fc * 128
            for dc in range(DC):
                E("tensor", "matmul", P8[:, kcol:kcol + 128],
                  wk_sb[:, dc * 1024 + fc * 128:dc * 1024 + fc * 128 + 128],
                  h1f_t[:, dc * 128:dc * 128 + 128],
                  start=(dc == 0), stop=False)
            E("tensor", "matmul", P8[:, kcol:kcol + 128],
              bk_row[0:1, fc * 128:fc * 128 + 128], onerow[0:1, 0:128],
              start=False, stop=True,
              inc=(TT.inc(1) if fc == DC - 1 else None))
        kwave = TT.at()
        E("scalar", "copy", kt_sb[:], P8[:, 2048:3072],
          wait=[kwave], inc=SS.inc(1))
        k_evac = SS.at()
        if _lim("A1W", 99) < 4:
            return
        for hh in range(2):
            for dc in range(DC):
                E("tensor", "matmul", bk(6 + hh),
                  h1f_t[:, dc * 128:dc * 128 + 128],
                  wv_sb[:, dc * 1024 + hh * 512:dc * 1024 + (hh + 1) * 512],
                  start=(dc == 0), stop=False)
            E("tensor", "matmul", bk(6 + hh), onerow[0:1, 0:128],
              bv_row[0:1, hh * 512:(hh + 1) * 512], start=False, stop=True,
              inc=(TT.inc(1) if hh == 1 else None))
        vwave = TT.at()
        E("vector", "tensor_copy", vaug_v[:, :, 0:64],
          P8[:, 3072:4096].rearrange("p (h c) -> p h c", c=64),
          wait=[vwave], inc=VV.inc(1))
        v_evac = VV.at()
        if _lim("A1W", 99) < 5:
            return
        for fc in range(DC):
            E("tensor", "matmul", P8[:, fc * 128:fc * 128 + 128], proj2[:],
              qt_sb[:, fc * 128:fc * 128 + 128], start=True, stop=True,
              wait=([q_evac] if fc == 0 else None),
              inc=(TT.inc(1) if fc == DC - 1 else None))
        qproj = TT.at()
        E("scalar", "activation", phi_e[:], P8[:, 0:1024],
          AF.Exp, wait=[qproj], inc=SS.inc(1))
        E("vector", "tensor_scalar", phi_r[:], P8[:, 0:1024],
          0.0, None, alu.max, wait=[SS.at()])
        E("vector", "scalar_tensor_tensor", Q_sb[:], phi_e[:], 1.0,
          phi_r[:], alu.min, alu.add, inc=VV.inc(1))
        q_phi = VV.at()
        if _lim("A1W", 99) < 6:
            return
        for fc in range(DC):
            E("tensor", "matmul", P8[:, 1024 + fc * 128:1024 + fc * 128 + 128],
              proj2[:],
              kt_sb[:, fc * 128:fc * 128 + 128], start=True, stop=True,
              wait=([k_evac] if fc == 0 else None),
              inc=(TT.inc(1) if fc == DC - 1 else None))
        kproj = TT.at()
        E("scalar", "activation", phi_k_e[:], P8[:, 1024:2048],
          AF.Exp, wait=[kproj], inc=SS.inc(1))
        E("vector", "tensor_scalar", phi_k_r[:], P8[:, 1024:2048],
          0.0, None, alu.max, wait=[SS.at()])
        E("vector", "scalar_tensor_tensor", K_sb[:], phi_k_e[:], 1.0,
          phi_k_r[:], alu.min, alu.add, inc=VV.inc(1))
        k_phi = VV.at()
        if _lim("A1W", 99) < 7:
            return
        for fc in range(DC):
            E("tensor", "transpose", bkb(0)[:, fc * 128:fc * 128 + 128],
              K_sb[:, fc * 128:fc * 128 + 128], ident_b[:],
              wait=([k_phi] if fc == 0 else None),
              inc=(TT.inc(1) if fc == DC - 1 else None))
        E("scalar", "copy", Ktm_sb[:], bkb(0)[:, 0:1024], wait=[TT.at()],
          inc=SS.inc(1))
        ktm_done = SS.at()
        if _lim("A1W", 99) < 8:
            return
        for rnd in range(2):
            for hh8 in range(8):
                hh = rnd * 8 + hh8
                E("tensor", "matmul", bk(hh8)[:, 0:128],
                  K_sb[64 * (hh % 2):64 * (hh % 2) + 64,
                       (hh // 2) * 128:(hh // 2) * 128 + 128],
                  Q_sb[64 * (hh % 2):64 * (hh % 2) + 64,
                       (hh // 2) * 128:(hh // 2) * 128 + 128],
                  start=True, stop=True,
                  wait=([ktm_done] if (rnd == 0 and hh8 == 0) else
                        ([VV.at()] if hh8 == 0 else None)),
                  inc=(TT.inc(1) if hh8 == 7 else None))
            aw = TT.at()
            E("vector", "scalar_tensor_tensor",
              A_sb[:, rnd * 1024:(rnd + 1) * 1024].rearrange(
                  "p (b c) -> p b c", c=128),
              banks_head(128), 1.0,
              tri_mask[:].unsqueeze(1).broadcast_to((128, 8, 128)),
              alu.bypass, alu.mult, wait=[aw], inc=VV.inc(1))
        aevac = VV.at()
        if _lim("A1W", 99) < 9:
            return
        for rnd in range(2):
            for hh8 in range(8):
                hh = rnd * 8 + hh8
                E("tensor", "matmul", bk(hh8)[:, 0:65],
                  A_sb[:, hh * 128:hh * 128 + 128],
                  V_aug[:, hh * 65:hh * 65 + 65],
                  start=True, stop=False,
                  wait=([aevac] if (rnd == 0 and hh8 == 0) else
                        ([VV.at()] if hh8 == 0 else None)))
                E("tensor", "matmul", bk(hh8)[:, 0:65],
                  Q_sb[64 * (hh % 2):64 * (hh % 2) + 64,
                       (hh // 2) * 128:(hh // 2) * 128 + 128],
                  S_b[64 * (hh % 2):64 * (hh % 2) + 64,
                      hh * 65:hh * 65 + 65],
                  start=False, stop=True,
                  inc=(TT.inc(1) if hh8 == 7 else None))
            nw = TT.at()
            E("vector", "tensor_copy",
              num_sb[:, rnd * 520:(rnd + 1) * 520].rearrange(
                  "p (b c) -> p b c", c=65),
              banks_head(65), wait=[nw], inc=VV.inc(1))
        numv = num_sb[:].rearrange("p (b c) -> p b c", c=65)
        E("vector", "tensor_copy", den_sb[:], numv[:, :, 64:65].squeeze(2))
        E("vector", "tensor_scalar", den_sb[:], den_sb[:], 1e-6, None,
          alu.add)
        E("vector", "reciprocal", rec_sb[:], den_sb[:])
        E("vector", "tensor_tensor",
          phi_e[:].rearrange("p (h c) -> p h c", c=64),
          numv[:, :, 0:64],
          rec_sb[:].unsqueeze(2).broadcast_to((128, 16, 64)),
          alu.mult)
        E("vector", "tensor_copy", atn_b[:], phi_e[:], inc=VV.inc(1))
        attn_done = VV.at()
        if _lim("A1W", 99) < 10:
            return
        for rnd in range(2):
            for pj4 in range(4):
                pj = rnd * 4 + pj4
                for par in range(2):
                    hh = 2 * pj + par
                    E("tensor", "matmul", bk(2 * pj4 + par)[:, 0:65],
                      Ktm_sb[:, pj * 128:pj * 128 + 128],
                      V_aug[:, hh * 65:hh * 65 + 65],
                      start=True, stop=True,
                      wait=([attn_done] if (rnd == 0 and pj4 == 0 and par == 0)
                            else ([VV.at()] if (pj4 == 0 and par == 0)
                                  else None)),
                      inc=(TT.inc(1) if (pj4 == 3 and par == 1) else None))
            dw = TT.at()
            E("vector", "tensor_add",
              S_sb[:, rnd * 520:(rnd + 1) * 520].rearrange(
                  "p (b c) -> p b c", c=65),
              S_sb[:, rnd * 520:(rnd + 1) * 520].rearrange(
                  "p (b c) -> p b c", c=65),
              banks_head(65), wait=[dw], inc=VV.inc(1))
        E("vector", "tensor_copy", S_b[:], S_sb[:], inc=VV.inc(1))
        supd = VV.at()
        if _lim("A1W", 99) < 11:
            return
        for fc in range(DC):
            E("tensor", "transpose", bkb(0)[:, fc * 128:fc * 128 + 128],
              atn_b[:, fc * 128:fc * 128 + 128], ident_b[:],
              wait=([supd] if fc == 0 else None),
              inc=(TT.inc(1) if fc == DC - 1 else None))
        E("scalar", "copy", at_fm[:], bkb(0)[:, 0:1024], wait=[TT.at()],
          inc=SS.inc(1))
        atfm = SS.at()
        if _lim("A1W", 99) < 12:
            return
        for hh in range(2):
            for dc in range(DC):
                E("tensor", "matmul", bk(hh),
                  at_fm[:, dc * 128:dc * 128 + 128],
                  wo_sb[:, dc * 1024 + hh * 512:dc * 1024 + (hh + 1) * 512],
                  start=(dc == 0), stop=(dc == DC - 1),
                  wait=([atfm, a1_wo_gd] if (hh == 0 and dc == 0) else None))
        E("tensor", "drain", inc=TT.inc(1))
        wopsum = TT.at()
        wl = [wopsum] + ([astore["m"]] if astore["m"] else [])
        E("vector", "tensor_add", a_t[:, 0:512], xn_t[:, 0:512], bk(0),
          wait=wl)
        E("vector", "tensor_add", a_t[:, 512:1024], xn_t[:, 512:1024],
          bk(1), inc=VV.inc(1))
        E("sync", "dma_start", out=a_scr_t[gtile], in_=a_t[:],
          wait=[VV.at()], inc=LD.inc(16))
        astore["m"] = LD.at()

    for seq in range(_lim("OB", OB)):
        E("vector", "memset", S_sb[:], 0.0)
        E("vector", "memset", S_b[:], 0.0, inc=VV.inc(1))
        for ch in range(_lim("NCHUNK", NCHUNK)):
            a1_chunk(seq * NCHUNK + ch)
    a1_last_ld = LD.at()

    exchange_consume(x1_gg)
    scale_pipeline(0, sx_b, inv_sx_b)
    sx_ld = LD.at()
    if KDEBUG:
        E("sync", "dma_start", out=dbg["dbg_sc1"], in_=sc_x[0:1, 0:4],
          wait=[sx_ld], inc=ST.inc(16))
    E("vector", "scalar_tensor_tensor", rs_all[:], stats_r[:], 1.0,
      inv_sx_b[:, 0:1].broadcast_to((128, 256)), alu.bypass, alu.mult,
      wait=[sx_ld])
    E("vector", "tensor_mul", mrs_all[:], stats_m[:], rs_all[:])
    E("vector", "tensor_scalar", nmrs_all[:], mrs_all[:], -1.0, None,
      alu.mult, inc=VV.inc(1))
    x1_vv = VV.at()

    if KDEBUG:
        E("sync", "dma_start", out=dbg["dbg_stats_m"], in_=stats_m[:],
          wait=[x1_vv], inc=ST.inc(16))
        E("sync", "dma_start", out=dbg["dbg_stats_r"], in_=stats_r[:],
          inc=ST.inc(16))
        E("vector", "tensor_copy", st_t1[0:1, 0:1], m_loc[0:1, 0:1])
        E("vector", "tensor_copy", st_t1[0:1, 1:2], m_glob[0:1, 0:1],
          inc=VV.inc(1))
        E("sync", "dma_start", out=dbg["dbg_m1"], in_=st_t1[0:1, 0:4],
          wait=[VV.at()], inc=ST.inc(16))

    reg = Region()
    w1_sb = reg.al([128, DC * FF], BF16)
    a2_stage = reg.al([128, FF // 2], F32)
    a2_a = reg.al([128, 2048], F32)
    a2_h2 = reg.al([128, 2048], BF16)
    a2_h2f = reg.al([128, DC * 256], BF16)
    a2_g = reg.al([128, FC1 * 256], BF16)
    a2_st = reg.al([128, 8], F32)
    b1p_f = reg.al([1, FF], F32)

    E("sync", "dma_start", out=b1p_f[0:1, :], in_=b1row_i,
      wait=[a1_last_ld, VV.at()], inc=LD.inc(16))
    b1_ld = LD.at()
    w1t = w1.rearrange("(n p) (h f) -> n h p f", p=128, h=2)
    for ch in range(DC):
        for hf in range(2):
            E("sync", "dma_start", out=a2_stage[:], in_=w1t[ch, hf],
              wait=[VV.at()], inc=LD.inc(16))
            E("vector", "tensor_copy",
              w1_sb[:, ch * FF + hf * 2048:ch * FF + (hf + 1) * 2048],
              a2_stage[:], wait=[LD.at()], inc=VV.inc(1))
    w1cast = VV.at()
    for hh in range(8):
        for ch in range(DC):
            E("tensor", "matmul", bk(hh)[0:1, :],
              b2f_b[:, ch:ch + 1],
              w1_sb[:, ch * FF + hh * 512:ch * FF + (hh + 1) * 512],
              start=(ch == 0), stop=(ch == DC - 1),
              wait=([w1cast, VV.at()] if (hh == 0 and ch == 0) else None),
              inc=(TT.inc(1) if (hh == 7 and ch == DC - 1) else None))
    E("vector", "tensor_add", b1p_f[0:1, :], b1p_f[0:1, :],
      P8[0:1, 0:4096], wait=[TT.at(), b1_ld])
    E("vector", "tensor_copy", b1p_row[0:1, :], b1p_f[0:1, :],
      inc=VV.inc(1))
    for ch in range(DC):
        E("vector", "tensor_scalar", w1_sb[:, ch * FF:(ch + 1) * FF],
          w1_sb[:, ch * FF:(ch + 1) * FF], g2_sb[:, ch:ch + 1],
          None, alu.mult)
    E("vector", "drain", inc=VV.inc(1))

    p_scr_t = p_scr.rearrange("(t p) d -> t p d", p=128)
    g_scr_t = g_scr
    NT_A2 = ROWS2 // 256
    a2_gmark = {}

    for t in range(_lim("NT_A2", NT_A2)):
        wl = [VV.at()]
        if t >= 1 and (t - 1) in a2_gmark:
            wl.append(a2_gmark[t - 1])
        E("sync", "dma_start",
          out=a2_a[:].rearrange("p (a d) -> p a d", d=D),
          in_=a_scr_t[2 * t:2 * t + 2].rearrange("a p d -> p a d"), wait=wl, inc=LD.inc(16))
        ld = LD.at()
        E("scalar", "nop", wait=[ld, VV.at()])
        for j in range(2):
            aa = a2_a[:, j * D:(j + 1) * D]
            E("scalar", "activation", sq_waste[:], aa, AF.Square,
              accum_out=a2_st[:, 4 * j + 1:4 * j + 2], inc=SS.inc(1))
            E("vector", "reduce_sum", a2_st[:, 4 * j:4 * j + 1], aa, AX.X,
              wait=([ld] if j == 0 else None))
            E("vector", "tensor_scalar", a2_st[:, 4 * j:4 * j + 1],
              a2_st[:, 4 * j:4 * j + 1], 1.0 / D, None, alu.mult)
            E("vector", "tensor_scalar", a2_st[:, 4 * j + 1:4 * j + 2],
              a2_st[:, 4 * j + 1:4 * j + 2], 1.0 / D, None, alu.mult,
              wait=[SS.at()])
            E("vector", "tensor_mul", a2_st[:, 4 * j + 2:4 * j + 3],
              a2_st[:, 4 * j:4 * j + 1], a2_st[:, 4 * j:4 * j + 1])
            E("vector", "tensor_sub", a2_st[:, 4 * j + 1:4 * j + 2],
              a2_st[:, 4 * j + 1:4 * j + 2], a2_st[:, 4 * j + 2:4 * j + 3])
            E("vector", "tensor_scalar", a2_st[:, 4 * j + 1:4 * j + 2],
              a2_st[:, 4 * j + 1:4 * j + 2], EPS, None, alu.add,
              inc=VV.inc(1))
            E("scalar", "activation", a2_st[:, 4 * j + 1:4 * j + 2],
              a2_st[:, 4 * j + 1:4 * j + 2], AF.Sqrt, wait=[VV.at()],
              inc=SS.inc(1))
            E("vector", "reciprocal", a2_st[:, 4 * j + 3:4 * j + 4],
              a2_st[:, 4 * j + 1:4 * j + 2], wait=[SS.at()])
            E("vector", "tensor_scalar", a2_h2[:, j * D:(j + 1) * D], aa,
              a2_st[:, 4 * j:4 * j + 1], a2_st[:, 4 * j + 3:4 * j + 4],
              alu.subtract, alu.mult, inc=VV.inc(1))
        h2_done = VV.at()
        for fc in range(DC):
            for j in range(2):
                tb = fc * 2 + j
                E("tensor", "transpose", P8b[:, tb * 128:tb * 128 + 128],
                  a2_h2[:, j * D + fc * 128:j * D + fc * 128 + 128],
                  ident_b[:],
                  wait=([h2_done, SS.at()] if tb == 0 else None),
                  inc=(TT.inc(1) if tb == 15 else None))
        E("scalar", "copy", a2_h2f[:], P8b[:, 0:2048], wait=[TT.at()],
          inc=SS.inc(1))
        h2f_done = SS.at()
        ss_marks = {}
        for fc in range(FC1):
            bb = 2 + (fc % 6)
            wl2 = [h2f_done] if fc < 6 else [ss_marks[fc - 6]]
            for dc in range(DC):
                E("tensor", "matmul", bk(bb)[:, 0:256],
                  w1_sb[:, dc * FF + fc * 128:dc * FF + fc * 128 + 128],
                  a2_h2f[:, dc * 256:(dc + 1) * 256],
                  start=(dc == 0), stop=False,
                  wait=(wl2 if dc == 0 else None))
            E("tensor", "matmul", bk(bb)[:, 0:256],
              b1p_row[0:1, fc * 128:fc * 128 + 128], onerow[0:1, 0:256],
              start=False, stop=True, inc=TT.inc(1))
            E("scalar", "activation", a2_g[:, fc * 256:(fc + 1) * 256],
              bk(bb)[:, 0:256], AF.Gelu, wait=[TT.at()], inc=SS.inc(1))
            ss_marks[fc] = SS.at()
        gelu_done = SS.at()
        E("gpsimd", "dma_start",
          out=g_scr_t[:, :, t * 256:(t + 1) * 256].rearrange(
              "n p f -> p n f"),
          in_=a2_g[:].rearrange("p (n f) -> p n f", f=256),
          wait=[gelu_done], inc=GD.inc(16))
        a2_gmark[t] = GD.at()
    a2a_gd = GD.at()

    reg = Region()
    w2_sb = reg.al([128, FC1 * 1024], BF16)
    b_g = [reg.al([128, FC1 * 128], BF16) for _ in range(2)]
    b_a2x = reg.al([128, 2048], F32)
    b_a2a = reg.al([128, 2048], F32)

    E("gpsimd", "dma_start",
      out=w2_sb[:].rearrange("p (n f) -> p n f", f=1024),
      in_=w2.rearrange("(n p) f -> p n f", p=128), wait=[a2a_gd, VV.at()],
      inc=GD.inc(16))
    a2w_gd = GD.at()
    a2_pstore = {"m": None}
    gld = {}

    for t in range(_lim("NT_A2", NT_A2)):
        wl = [VV.at()]
        if t == 0:
            wl.append(a2a_gd)
        if a2_pstore["m"]:
            wl.append(a2_pstore["m"])
        E("sync", "dma_start",
          out=b_a2a[:].rearrange("p (a d) -> p a d", d=D),
          in_=a_scr_t[2 * t:2 * t + 2].rearrange("a p d -> p a d"), wait=wl, inc=LD.inc(16))
        E("sync", "dma_start",
          out=b_a2x[:].rearrange("p (a d) -> p a d", d=D),
          in_=x2t[2 * t:2 * t + 2].rearrange("a p d -> p a d"), inc=LD.inc(16))
        for j in range(2):
            bi = (2 * t + j) % 2
            wlg = [VV.at()] if (2 * t + j) >= 2 else [VV.at()]
            E("sync", "dma_start",
              out=b_g[bi][:].rearrange("p (n f) -> p n f", f=128),
              in_=g_scr_t[:, :, (2 * t + j) * 128:(2 * t + j + 1) * 128
                          ].rearrange("n p f -> p n f"),
              wait=wlg, inc=LD.inc(16))
        ld = LD.at()
        for j in range(2):
            bi = (2 * t + j) % 2
            for hh in range(2):
                for fc in range(FC1):
                    E("tensor", "matmul", bk(hh),
                      b_g[bi][:, fc * 128:(fc + 1) * 128],
                      w2_sb[:, fc * 1024 + hh * 512:fc * 1024 + (hh + 1) * 512],
                      start=(fc == 0), stop=(fc == FC1 - 1),
                      wait=([ld, a2w_gd, VV.at()]
                            if (hh == 0 and fc == 0) else None))
            E("tensor", "drain", inc=TT.inc(1))
            w2ps = TT.at()
            E("vector", "tensor_add", b_a2x[:, j * D:(j + 1) * D],
              b_a2x[:, j * D:(j + 1) * D], b2_bc[:], wait=[w2ps, b2bc_ld])
            E("vector", "tensor_add", b_a2x[:, j * D:(j + 1) * D],
              b_a2x[:, j * D:(j + 1) * D], b_a2a[:, j * D:(j + 1) * D])
            E("vector", "tensor_add", b_a2x[:, j * D:j * D + 512],
              b_a2x[:, j * D:j * D + 512], bk(0))
            E("vector", "tensor_add", b_a2x[:, j * D + 512:(j + 1) * D],
              b_a2x[:, j * D + 512:(j + 1) * D], bk(1), inc=VV.inc(1))
            E("sync", "dma_start", out=p_scr_t[2 * t + j],
              in_=b_a2x[:, j * D:(j + 1) * D], wait=[VV.at()],
              inc=LD.inc(16))
        a2_pstore["m"] = LD.at()
    a2_done = LD.at()

    reg = Region()
    wgT_sb = reg.al([128, DC * II], BF16)
    wuT_sb = reg.al([128, DC * II], BF16)
    b_x = [reg.al([128, 4096], F32) for _ in range(2)]
    b_xq = reg.al([128, 4096], BF16)
    b_xqf = reg.al([128, DC * 512], BF16)
    b_silu = [reg.al([128, 512], F32) for _ in range(2)]
    b_h = [reg.al([128, 512], F32) for _ in range(2)]
    b_hmx = reg.al([128, ICN], F32)

    E("sync", "dma_start", out=swg_sb[:], in_=swg_scr.rearrange("a p -> p a"),
      wait=[a2_done], inc=LD.inc(16), allow_slow_non_contiguous=True)
    E("sync", "dma_start", out=swu_sb[:], in_=swu_scr.rearrange("a p -> p a"),
      inc=LD.inc(16), allow_slow_non_contiguous=True)
    E("sync", "dma_start",
      out=wgT_sb[:].rearrange("p (n f) -> p n f", f=II),
      in_=wgT_scr.rearrange("n p f -> p n f"), inc=LD.inc(16))
    E("sync", "dma_start",
      out=wuT_sb[:].rearrange("p (n f) -> p n f", f=II),
      in_=wuT_scr.rearrange("n p f -> p n f"), inc=LD.inc(16))
    bw_ld = LD.at()
    E("vector", "scalar_tensor_tensor", scg_sb[:], swg_sb[:], 1.0,
      sx_b[:, 0:1].broadcast_to((128, ICN)), alu.bypass, alu.mult,
      wait=[bw_ld, x1_vv])
    E("vector", "scalar_tensor_tensor", scu_sb[:], swu_sb[:], 1.0,
      sx_b[:, 0:1].broadcast_to((128, ICN)), alu.bypass, alu.mult,
      inc=VV.inc(1))
    bsc_vv = VV.at()

    b_vmark = [None, None]
    b_simark = {}
    b_hmark = {}
    b_hstore = [None, None]
    NT_OWN = ROWS2 // 512

    for t in range(_lim("NT8B", NT8)):
        bi = t % 2
        own_tile = t < NT_OWN
        E("sync", "dma_start",
          out=b_x[bi][:].rearrange("p (a d) -> p a d", d=D),
          in_=x8t[4 * t:4 * t + 4].rearrange("a p d -> p a d"),
          wait=([b_vmark[bi]] if b_vmark[bi] else None), inc=LD.inc(16))
        ld = LD.at()
        E("scalar", "nop", wait=[ld, bsc_vv])
        for j in range(4):
            col = 4 * t + j
            xx = b_x[bi][:, j * D:(j + 1) * D]
            E("scalar", "activation", xx, xx, AF.Identity,
              scale=rs_all[:, col:col + 1], bias=nmrs_all[:, col:col + 1],
              inc=SS.inc(1))
            sj = SS.at()
            E("vector", "tensor_scalar", xx, xx, MAGIC, MAGIC, alu.add,
              alu.subtract, wait=[sj])
            E("vector", "tensor_scalar", b_xq[:, j * D:(j + 1) * D], xx,
              QMAX, -QMAX - 1.0, alu.min, alu.max)
        E("vector", "drain", inc=VV.inc(1))
        xq_done = VV.at()
        b_vmark[bi] = xq_done
        for dcp in range(4):
            for dc2 in range(2):
                dc = dcp * 2 + dc2
                for j in range(4):
                    tb = dc2 * 4 + j
                    E("tensor", "transpose",
                      bkb(2 + (dcp % 2))[:, tb * 128:tb * 128 + 128],
                      b_xq[:, j * D + dc * 128:j * D + dc * 128 + 128],
                      ident_b[:],
                      wait=([xq_done, VV.at()] if (dc2 == 0 and j == 0)
                            else None),
                      inc=(TT.inc(1) if (dc2 == 1 and j == 3) else None))
            E("vector", "tensor_copy",
              b_xqf[:, dcp * 1024:(dcp + 1) * 1024],
              bkb(2 + (dcp % 2))[:, 0:1024], wait=[TT.at()], inc=VV.inc(1))
        xqf_done = VV.at()
        for ic in range(ICN):
            pp = ic % 2
            wl_g = [xqf_done] if ic >= 2 else [xqf_done, bw_ld]
            if ic >= 2:
                wl_g.append(b_simark[ic - 2])
            for dc in range(DC):
                E("tensor", "matmul", bk(4 + pp),
                  wgT_sb[:, dc * II + ic * 128:dc * II + ic * 128 + 128],
                  b_xqf[:, dc * 512:(dc + 1) * 512],
                  start=(dc == 0), stop=(dc == DC - 1),
                  wait=(wl_g if dc == 0 else None))
            wl_u = [b_hmark[ic - 2]] if ic >= 2 else []
            for dc in range(DC):
                E("tensor", "matmul", bk(6 + pp),
                  wuT_sb[:, dc * II + ic * 128:dc * II + ic * 128 + 128],
                  b_xqf[:, dc * 512:(dc + 1) * 512],
                  start=(dc == 0), stop=(dc == DC - 1),
                  wait=(wl_u if dc == 0 else None))
            E("tensor", "drain", inc=TT.inc(1))
            u_ps = TT.at()
            wl_s = [u_ps]
            if ic >= 2:
                wl_s.append(b_hmark[ic - 2])
            E("scalar", "activation", b_silu[pp][:], bk(4 + pp), AF.Silu,
              scale=scg_sb[:, ic:ic + 1], wait=wl_s, inc=SS.inc(1))
            b_simark[ic] = SS.at()
            wl_h = [u_ps, SS.at()]
            if b_hstore[pp] is not None:
                wl_h.append(b_hstore[pp])
            E("vector", "scalar_tensor_tensor", b_h[pp][:], bk(6 + pp),
              scu_sb[:, ic:ic + 1], b_silu[pp][:], alu.mult, alu.mult,
              wait=wl_h)
            E("vector", "reduce_max", b_hmx[:, ic:ic + 1], b_h[pp][:],
              AX.X, apply_absolute_value=True, inc=VV.inc(1))
            b_hmark[ic] = VV.at()
            if own_tile:
                E("gpsimd", "dma_start",
                  out=h_scr[ic, :, t * 512:(t + 1) * 512],
                  in_=b_h[pp][:], wait=[VV.at()], inc=GD.inc(16))
                b_hstore[pp] = GD.at()
        E("vector", "tensor_max", hmax[:], hmax[:], b_hmx[:], inc=VV.inc(1))
        b_vmark[bi] = VV.at()
    E("vector", "reduce_max", gmax[:], hmax[:], AX.X, inc=VV.inc(1))
    fold_max()

    exchange_consume(pair_exchange(8, PL2, "x2"))
    scale_pipeline(1, sd_b, inv_sd_b)
    sd_ld = LD.at()
    if KDEBUG:
        E("sync", "dma_start", out=dbg["dbg_sc2"], in_=sc_x[0:1, 0:4],
          wait=[sd_ld], inc=ST.inc(16))

    if KDEBUG:
        E("vector", "tensor_copy", st_t1[0:1, 0:1], m_loc[0:1, 0:1])
        E("vector", "tensor_copy", st_t1[0:1, 1:2], m_glob[0:1, 0:1],
          inc=VV.inc(1))
        E("sync", "dma_start", out=dbg["dbg_m2"], in_=st_t1[0:1, 0:4],
          wait=[VV.at()], inc=ST.inc(16))

    reg = Region()
    wdT_sb = reg.al([128, ICN * 1024], BF16)
    c_h = [reg.al([128, ICN * 128], F32) for _ in range(2)]
    c_ih = [reg.al([128, ICN * 128], BF16) for _ in range(2)]
    c_p = [reg.al([128, 1024], F32) for _ in range(2)]
    c_o = [reg.al([128, 1024], F32) for _ in range(2)]
    swd_row = reg.al([1, 1024], F32)

    E("sync", "dma_start",
      out=wdT_sb[:].rearrange("p (n f) -> p n f", f=1024),
      in_=wdT_scr.rearrange("n p f -> p n f"),
      wait=[VV.at(), GD.at()], inc=LD.inc(16))
    E("sync", "dma_start",
      out=swd_row[0:1, :].rearrange("q (a p) -> q a p", p=128),
      in_=swd_scr.unsqueeze(0), inc=LD.inc(16))
    E("sync", "dma_start", out=swd_bc[0:1, :], in_=swd_row[0:1, :],
      wait=[LD.at()], inc=LD.inc(16))
    k = 1
    while k < 128:
        kk = min(k, 128 - k)
        E("sync", "dma_start", out=swd_bc[k:k + kk, :], in_=swd_bc[0:kk, :],
          wait=[LD.at()], inc=LD.inc(16))
        k *= 2
    cw_ld = LD.at()
    E("vector", "scalar_tensor_tensor", swd_bc[:], swd_bc[:], 1.0,
      sd_b[:, 0:1].broadcast_to((128, D)), alu.bypass, alu.mult,
      wait=[cw_ld, sd_ld], inc=VV.inc(1))
    cswd_vv = VV.at()

    y_t = y.rearrange("b (t p) d -> (b t) p d", p=128)
    NT_C = ROWS2 // 128
    c_vmark = [None, None]
    c_stmark = [None, None]
    c_tmark = [None, None]

    for t in range(_lim("NT_C", NT_C)):
        bi = t % 2
        wl = [c_vmark[bi]] if c_vmark[bi] else []
        if t == 0:
            wl.append(GD.at())
        E("sync", "dma_start",
          out=c_h[bi][:].rearrange("p (n f) -> p n f", f=128),
          in_=h_scr[:, :, t * 128:(t + 1) * 128].rearrange(
              "n p f -> p n f"),
          wait=wl, inc=LD.inc(16))
        E("sync", "dma_start", out=c_p[bi][:], in_=p_scr_t[t],
          inc=LD.inc(16))
        ld = LD.at()
        wlq = [ld, cswd_vv]
        if c_tmark[bi]:
            wlq.append(c_tmark[bi])
        E("vector", "scalar_tensor_tensor", c_h[bi][:], c_h[bi][:], 1.0,
          inv_sd_b[:, 0:1].broadcast_to((128, ICN * 128)), alu.bypass,
          alu.mult, wait=wlq)
        E("vector", "tensor_scalar", c_h[bi][:], c_h[bi][:], MAGIC, MAGIC,
          alu.add, alu.subtract)
        E("vector", "tensor_scalar", c_ih[bi][:], c_h[bi][:], QMAX,
          -QMAX - 1.0, alu.min, alu.max, inc=VV.inc(1))
        ih_done = VV.at()
        for hh in range(2):
            for ic in range(ICN):
                E("tensor", "matmul", bk(hh),
                  c_ih[bi][:, ic * 128:(ic + 1) * 128],
                  wdT_sb[:, ic * 1024 + hh * 512:ic * 1024 + (hh + 1) * 512],
                  start=(ic == 0), stop=(ic == ICN - 1),
                  wait=([ih_done, cw_ld] if (hh == 0 and ic == 0) else None))
        E("tensor", "drain", inc=TT.inc(1))
        dn_ps = TT.at()
        c_tmark[bi] = dn_ps
        wl2 = [dn_ps]
        if c_stmark[bi]:
            wl2.append(c_stmark[bi])
        E("vector", "scalar_tensor_tensor", c_o[bi][:, 0:512], bk(0), 1.0,
          swd_bc[:, 0:512], alu.bypass, alu.mult, wait=wl2)
        E("vector", "scalar_tensor_tensor", c_o[bi][:, 512:1024], bk(1),
          1.0, swd_bc[:, 512:1024], alu.bypass, alu.mult)
        E("vector", "tensor_add", c_o[bi][:], c_o[bi][:], c_p[bi][:],
          inc=VV.inc(1))
        c_vmark[bi] = VV.at()
        E("sync", "dma_start", out=y_t[t], in_=c_o[bi][:], wait=[VV.at()],
          inc=ST.inc(16))
        c_stmark[bi] = ST.at()

    if KDEBUG:
        E("sync", "dma_start", out=dbg["dbg_a0"], in_=a_scr_t[0],
          wait=[ST.at()], inc=ST.inc(16))
        E("sync", "dma_start", out=dbg["dbg_a31"], in_=a_scr_t[31],
          inc=ST.inc(16))
        E("sync", "dma_start", out=dbg["dbg_p0"], in_=p_scr_t[0],
          inc=ST.inc(16))
        E("sync", "dma_start", out=dbg["dbg_h0"], in_=h_scr[:, :, 0:128],
          inc=ST.inc(16))
        E("sync", "dma_start", out=dbg["dbg_g0"], in_=g_scr[:, :, 0:128],
          inc=ST.inc(16))

    fin = ST.at()
    streams["sync"].append(lambda e, _f=fin: e.wait_ge(*_f))

    with nc.Block() as block:
        @block.sync
        def _(e):
            for f in streams["sync"]:
                f(e)

        @block.vector
        def _(e):
            for f in streams["vector"]:
                f(e)

        @block.scalar
        def _(e):
            for f in streams["scalar"]:
                f(e)

        @block.tensor
        def _(e):
            for f in streams["tensor"]:
                f(e)

        @block.gpsimd
        def _(e):
            for f in streams["gpsimd"]:
                f(e)

    return nc


_NC_CACHE = None


def kernel(L0_attn_x, L0_attn_wq, L0_attn_wk, L0_attn_wv, L0_attn_wo,
           L0_attn_proj, L0_attn_w1, L0_attn_b1, L0_attn_w2, L0_attn_b2,
           L0_attn_g1, L0_attn_beta1, L0_attn_g2, L0_attn_beta2,
           L0_ffn_clip_x, L0_ffn_clip_down, L0_ffn_w_gate, L0_ffn_w_up,
           L0_ffn_w_down):
    global _NC_CACHE
    if _NC_CACHE is None:
        _NC_CACHE = build_nc()
    nc = _NC_CACHE

    x = np.ascontiguousarray(np.asarray(L0_attn_x, np.float32))
    f32 = lambda a: np.ascontiguousarray(np.asarray(a, np.float32))
    com = dict(
        wq=f32(L0_attn_wq), wk=f32(L0_attn_wk), wv=f32(L0_attn_wv),
        wo=f32(L0_attn_wo), proj=f32(L0_attn_proj), w1=f32(L0_attn_w1),
        w2=f32(L0_attn_w2), wg=f32(L0_ffn_w_gate), wu=f32(L0_ffn_w_up),
        wd=f32(L0_ffn_w_down),
        g1T=np.ascontiguousarray(f32(L0_attn_g1).reshape(DC, 128).T),
        beta1T=np.ascontiguousarray(f32(L0_attn_beta1).reshape(DC, 128).T),
        g2T=np.ascontiguousarray(f32(L0_attn_g2).reshape(DC, 128).T),
        beta2T=np.ascontiguousarray(f32(L0_attn_beta2).reshape(DC, 128).T),
        b1row=f32(L0_attn_b1).reshape(1, FF),
        b2row=f32(L0_attn_b2).reshape(1, D),
        clip_x=f32(L0_ffn_clip_x).reshape(1, 1),
        clip_dn=f32(L0_ffn_clip_down).reshape(1, 1),
    )
    nz = np.frombuffer(os.urandom(8), dtype=np.int32).reshape(1, 2).copy()
    nz |= 1
    in_maps = []
    for c in range(NC):
        sb = _stat_batches(c)
        in_maps.append(dict(com, x8=np.ascontiguousarray(x[sb]), nonce=nz))
    res = run_bass_kernel_spmd(nc, in_maps, core_ids=list(range(NC)))
    out = np.empty((B, S, D), np.float32)
    for c in range(NC):
        out[2 * c:2 * c + 2] = res.results[c]["y"]
    return out
